# version 23
# baseline (speedup 1.0000x reference)
import numpy as np
import ml_dtypes
from contextlib import ExitStack

import concourse.bass as bass
import concourse.mybir as mybir
from concourse.bass_utils import run_bass_kernel_spmd

F32 = mybir.dt.float32
BF16 = mybir.dt.bfloat16
AF = mybir.ActivationFunctionType
ALU = mybir.AluOpType
AX = mybir.AxisListType

N_CORES = 8
DM = 1024
NTOK = 4096
NTILE = 32
TCB = 1024
NTB = 8
NBCHUNK = 4
NEXP = 16
EI = 511
SI = 1023
LN_EPS = 1e-5
ATT_EPS = 1e-6
ESLOT = 12276

PHASE_B = True
_DBG_CHUNK = -1
_DBG_B = -1


class _Dummy:
    def then_inc(self, *a, **k):
        return self

_DUM = _Dummy()


class EngP:
    def __init__(self, eng, active):
        self._eng = eng
        self._act = active

    def __getattr__(self, name):
        if self._act:
            return getattr(self._eng, name)
        return lambda *a, **k: _DUM


class Sem:
    __slots__ = ("h", "n")

    def __init__(self, h):
        self.h = h
        self.n = 0


def I(sem, instr, k=1):
    sem.n += k
    if instr is not _DUM:
        instr.then_inc(sem.h, k)
    return sem.n


def W(eng, sem, val=None):
    v = sem.n if val is None else val
    if v > 0:
        eng.wait_ge(sem.h, v)


class Arena:
    def __init__(self, tensor, nelem):
        self.t = tensor
        self.off = 0
        self.n = nelem

    def take(self, elems, align=64):
        self.off = (self.off + align - 1) // align * align
        v = self.t[:, self.off:self.off + elems]
        self.off += elems
        assert self.off <= self.n, f"arena overflow {self.off} > {self.n}"
        return v


def build_program(phase_b=PHASE_B):
    nc = bass.Bass("TRN2", target_bir_lowering=False,
                   detect_race_conditions=False)
    ap = {}

    def din(name, shape, dtype=F32):
        ap[name] = nc.dram_tensor(name, shape, dtype, kind="ExternalInput").ap()

    din("x", [NTOK, DM])
    for w in ("wq_t", "wk_t", "wv_t", "wp_t"):
        din(w, [DM, DM])
    din("bp_bc", [128, DM])
    din("gw_t", [DM, NEXP])
    din("we1_t", [NEXP, DM, EI])
    din("we3_t", [NEXP, DM, EI])
    din("we2p_t", [NEXP, EI + 1, SI])
    din("be1_l", [128, NEXP * 4])
    din("be3_l", [128, NEXP * 4])
    din("btw5e", [NEXP, 4, 5, SI])
    din("ws1_t", [DM, SI])
    din("ws3_t", [DM, SI])
    din("ws2p_t", [DM, SI])
    din("bs1_l", [128, 8])
    din("bs3_l", [128, 8])
    din("btw5s", [4, 5, SI])
    din("bt5_init", [5, 256], BF16)
    din("mask_ut", [128, 128])
    din("ident_bf", [128, 128], BF16)
    din("onescol", [128, 2], BF16)

    ap["out"] = nc.dram_tensor("out", [NTOK, DM], F32,
                               kind="ExternalOutput").ap()
    if _DBG_CHUNK >= 0:
        for nm, dt in (("d_ln1b", BF16), ("d_lnf", BF16), ("d_qf", BF16),
                       ("d_kf", BF16), ("d_ktm", BF16), ("d_vtm", BF16),
                       ("d_ytm", BF16), ("d_atb", BF16)):
            ap[nm] = nc.dram_tensor(nm, [128, 2048], dt,
                                    kind="ExternalOutput").ap()
    if _DBG_B >= 0:
        for nm in ("db_y", "db_xtf", "db_h"):
            ap[nm] = nc.dram_tensor(nm, [128, NTB * DM], F32,
                                    kind="ExternalOutput").ap()
        ap["db_st"] = nc.dram_tensor("db_st", [128, 64], F32,
                                     kind="ExternalOutput").ap()
        ap["db_timeb"] = nc.dram_tensor("db_timeb", [128, 8], F32,
                                        kind="ExternalOutput").ap()

        ap["db_gate"] = nc.dram_tensor("db_gate", [128, NTILE * NEXP], F32,
                                       kind="ExternalOutput").ap()
        ap["db_ue"] = nc.dram_tensor("db_ue", [128, 8], F32,
                                     kind="ExternalOutput").ap()
    ap["xprime"] = nc.dram_tensor("xprime", [NTOK, DM], F32,
                                  kind="Internal").ap()
    ap["ln3s"] = nc.dram_tensor("ln3s", [DM, NTOK], BF16,
                                kind="Internal").ap()

    with ExitStack() as es:
        E = es.enter_context
        sb = {}

        def S_(name, shape, dtype):
            sb[name] = E(nc.sbuf_tensor(name, shape, dtype))
            return sb[name]

        S_("gate", [128, NTILE * NEXP], F32)
        S_("maskut", [128, 128], F32)
        S_("ident", [128, 128], BF16)
        S_("onesc", [128, 2], BF16)
        S_("gw", [128, 8 * NEXP], BF16)
        S_("be1", [128, NEXP * 4], F32)
        S_("be3", [128, NEXP * 4], F32)
        S_("bs1", [128, 8], F32)
        S_("bs3", [128, 8], F32)
        S_("btw5x", [5, 2 * 4 * SI], BF16)
        S_("bt5", [5, 256], BF16)
        S_("onesf", [128, 8], F32)
        for i in range(16):
            S_("st%x" % i, [128, 16], F32)
        S_("v2c", [128, 8], F32)
        S_("dotc", [128, 8], F32)
        S_("lam", [128, 8], F32)
        S_("lami", [128, 8], F32)
        S_("ue", [128, 8], F32)
        S_("timeb", [128, 8], BF16)
        
        S_("Zf", [128, 8], F32)
        S_("Zb", [128, 8], BF16)
        S_("ceps5", [128, 1], F32)
        S_("ceps6", [128, 1], F32)
        S_("cone", [128, 1], F32)
        S_("cneg6", [128, 1], F32)

        AB_N = 50688
        AF_N = 21632
        ab_t = S_("arena_bf", [128, AB_N], BF16)
        af_t = S_("arena_f32", [128, AF_N], F32)

        PS = [E(nc.psum_tensor("ps%d" % i, [128, 512], F32)) for i in range(7)]
        PT = E(nc.psum_tensor("pst", [128, 1024], BF16))

        sem_names = ["sdt", "sdx", "sds", "sdr", "gd", "ts", "vs", "ss"]
        sem_h = {n: E(nc.semaphore(n)) for n in sem_names}
        block = E(nc.Block())

        def script(cur, engs):
            gp = EngP(engs.get("gpsimd"), cur == "gpsimd")
            sc = EngP(engs.get("scalar"), cur == "scalar")
            te = EngP(engs.get("tensor"), cur == "tensor")
            ve = EngP(engs.get("vector"), cur == "vector")
            sy = EngP(engs.get("sync"), cur == "sync")
            sm = {n: Sem(sem_h[n]) for n in sem_names}
            sdt, sdx, sds, sdr, gd, ts, vs, ss = (sm[n] for n in sem_names)
            guard = {}

            def db(eng, sem):
                if sem.n > 0:
                    W(eng, sem, sem.n)

            def G(eng, key):
                if key in guard:
                    s, n = guard[key]
                    W(eng, s, n)

            def setg(key, sem):
                guard[key] = (sem, sem.n)

            def FV(ins):
                I(vs, ins)
                W(ve, vs)

            A = Arena(ab_t, AB_N)
            Fa = Arena(af_t, AF_N)
            wq3 = A.take(8 * DM).rearrange("p (k m) -> p k m", k=8)
            wk3 = A.take(8 * DM).rearrange("p (k m) -> p k m", k=8)
            wv3 = A.take(8 * DM).rearrange("p (k m) -> p k m", k=8)
            wp3 = A.take(8 * DM).rearrange("p (k m) -> p k m", k=8)
            ln1b = A.take(2 * DM)
            lnf_a = A.take(8 * 128)
            lnf3a = lnf_a.rearrange("p (k n) -> p k n", k=8)
            lnf_b = A.take(8 * 128)
            lnf3b = lnf_b.rearrange("p (k n) -> p k n", k=8)
            g_lnf = A.take(8 * 128)
            g_lnf3 = g_lnf.rearrange("p (k n) -> p k n", k=8)
            qf = A.take(8 * 128)
            qf3 = qf.rearrange("p (k n) -> p k n", k=8)
            kf = A.take(8 * 128)
            kf3 = kf.rearrange("p (k n) -> p k n", k=8)
            ktm = A.take(DM)
            vtm = A.take(DM)
            atb = A.take(16 * 128)
            atb3 = atb.rearrange("p (h n) -> p h n", h=16)
            ytm = A.take(2 * DM)
            yfm = A.take(8 * 128)
            yfm3 = yfm.rearrange("p (k n) -> p k n", k=8)
            ln3b = A.take(2 * DM)
            Sb = A.take(512)

            xb = Fa.take(4 * DM)
            psb = Fa.take(DM)
            tmpa = Fa.take(DM)
            tmpb = Fa.take(DM)
            tmpc = Fa.take(DM)
            junk = Fa.take(DM)
            xpb = Fa.take(2 * DM)
            Sf = Fa.take(512)
            glog = Fa.take(NEXP)
            bpbc = Fa.take(DM)

            gate = sb["gate"][:, :]
            maskut = sb["maskut"][:, :]
            ident = sb["ident"][:, :]
            onesc = sb["onesc"][:, :]
            gw3 = sb["gw"][:, :].rearrange("p (k n) -> p k n", k=8)
            st = [sb["st%x" % i][:, :] for i in range(16)]
            Zf = sb["Zf"][:, :]
            Zb = sb["Zb"][:, :]
            ceps5 = sb["ceps5"][:, 0:1]
            ceps6 = sb["ceps6"][:, 0:1]
            cone = sb["cone"][:, 0:1]
            cneg6 = sb["cneg6"][:, 0:1]

            def sc_ln_stats(sum_col, ss_col, m_out, rstd_out, t0, t1):
                sc.mul(m_out, sum_col, 1.0 / DM)
                sc.mul(t0, ss_col, 1.0 / DM)
                sc.square(t1, m_out)
                sc.activation(t1, t1, AF.Identity, scale=-1.0, bias=ceps5)
                sc.activation(t0, t0, AF.Ln, bias=t1)
                return sc.activation(rstd_out, t0, AF.Exp, scale=-0.5)

            for w3, src in ((wq3, "wq_t"), (wk3, "wk_t"), (wv3, "wv_t"),
                            (wp3, "wp_t")):
                ins = gp.dma_start(
                    out=w3, in_=ap[src].rearrange("(k p) m -> p k m", p=128))
                I(gd, ins, 16)
            ins = gp.dma_start(
                out=gw3, in_=ap["gw_t"].rearrange("(k p) n -> p k n", p=128))
            I(gd, ins, 16)
            ins = gp.dma_start(out=sb["bt5"][:, :], in_=ap["bt5_init"])
            I(gd, ins, 16)
            for nm_, src in (("maskut", "mask_ut"), ("ident", "ident_bf"),
                             ("onesc", "onescol"),
                             ("be1", "be1_l"), ("be3", "be3_l"),
                             ("bs1", "bs1_l"), ("bs3", "bs3_l")):
                ins = sy.dma_start(out=sb[nm_][:, :], in_=ap[src])
                I(sds, ins, 16)
            ins = sy.dma_start(out=bpbc, in_=ap["bp_bc"])
            I(sds, ins, 16)
            pre_sd = sds.n
            pre_gd = gd.n
            ve.memset(sb["onesf"][:, :], 1.0)
            ve.memset(sb["ceps5"][:, :], LN_EPS)
            ve.memset(sb["ceps6"][:, :], ATT_EPS)
            ve.memset(sb["cone"][:, :], 1.0)
            ins = ve.memset(sb["cneg6"][:, :], -ATT_EPS)
            I(vs, ins)

            ins = sy.dma_start(out=xb[:, 0:DM], in_=ap["x"][0:128, :])
            I(sdx, ins, 16)
            xload = {0: sdx.n}

            W(te, sds, pre_sd)
            W(te, gd, pre_gd)
            W(ve, sds, pre_sd)
            W(sc, sds, pre_sd)
            W(sc, vs)

            lnb_done, lnf_ev, y_ev, ln3_done = {}, {}, {}, {}

            def x_view(c):
                return xb[:, (c % 4) * DM:(c % 4) * DM + DM]

            def emit_pre(c):
                if c + 1 < NTILE:
                    G(sy, "xb%d" % ((c + 1) % 4))
                    db(sy, sdx)
                    ins = sy.dma_start(
                        out=x_view(c + 1),
                        in_=ap["x"][(c + 1) * 128:(c + 2) * 128, :])
                    xload[c + 1] = I(sdx, ins, 16)
                xv = x_view(c)
                lv = ln1b[:, (c % 2) * DM:(c % 2) * DM + DM]
                lf3 = lnf3a if c % 2 == 0 else lnf3b
                W(ve, sdx, xload[c])
                ve.scalar_tensor_tensor(junk, xv, 1.0, xv, ALU.mult, ALU.mult,
                                        accum_out=st[0][:, 0:1])
                ins = ve.tensor_reduce(st[0][:, 1:2], xv, AX.X, ALU.add)
                I(vs, ins)
                W(sc, vs)
                ins = sc_ln_stats(st[0][:, 1:2], st[0][:, 0:1],
                                  st[0][:, 2:3], st[0][:, 3:4],
                                  st[0][:, 4:5], st[0][:, 5:6])
                I(ss, ins)
                W(ve, ss)
                G(ve, "ln1b%d" % (c % 2))
                ins = ve.tensor_scalar(lv, xv, st[0][:, 2:3], st[0][:, 3:4],
                                       ALU.subtract, ALU.mult)
                lnb_done[c] = I(vs, ins)

            def emit_lnf_tp(c):
                lv = ln1b[:, (c % 2) * DM:(c % 2) * DM + DM]
                lf = lnf_a if c % 2 == 0 else lnf_b
                G(te, "PT")
                W(te, vs, lnb_done[c])
                for j in range(8):
                    ins = te.transpose(PT[:, j * 128:(j + 1) * 128],
                                       lv[:, j * 128:(j + 1) * 128], ident)
                tpn = I(ts, ins)
                setg("ln1b%d" % (c % 2), ts)
                W(sc, ts, tpn)
                G(sc, "lnf%d" % (c % 2))
                sc.copy(lf[:, 0:512], PT[:, 0:512])
                ins = sc.copy(lf[:, 512:1024], PT[:, 512:1024])
                lnf_ev[c] = I(ss, ins)
                setg("PT", ss)

            def emit_qkv(c):
                lf3 = lnf3a if c % 2 == 0 else lnf3b
                W(te, ss, lnf_ev[c])
                for half in range(2):
                    b = (2, 4)[half]
                    G(te, "P%d" % b)
                    for mi in range(4):
                        m = half * 4 + mi
                        for k in range(8):
                            ins = te.matmul(
                                PS[b][:, mi * 128:(mi + 1) * 128],
                                wk3[:, k, m * 128:(m + 1) * 128],
                                lf3[:, k, :], start=(k == 0), stop=(k == 7))
                    kn = I(ts, ins)
                    tmp = tmpa if half == 0 else tmpb
                    W(sc, ts, kn)
                    G(sc, "phitmp%d" % half)
                    sc.activation(tmp[:, 0:512], PS[b][:, 0:512], AF.Relu,
                                  scale=-1.0)
                    ins = sc.activation(tmp[:, 0:512], tmp[:, 0:512], AF.Exp,
                                        scale=-1.0)
                    I(ss, ins)
                    W(ve, ss)
                    G(ve, "kf")
                    ins = ve.scalar_tensor_tensor(
                        kf[:, half * 512:half * 512 + 512], PS[b][:, 0:512],
                        0.0, tmp[:, 0:512], ALU.max, ALU.add)
                    I(vs, ins)
                    setg("P%d" % b, vs)
                    setg("phitmp%d" % half, vs)
                kf_done = vs.n
                for half in range(2):
                    b = (0, 1)[half]
                    G(te, "P%d" % b)
                    for mi in range(4):
                        m = half * 4 + mi
                        for k in range(8):
                            ins = te.matmul(
                                PS[b][:, mi * 128:(mi + 1) * 128],
                                wq3[:, k, m * 128:(m + 1) * 128],
                                lf3[:, k, :], start=(k == 0), stop=(k == 7))
                    qn = I(ts, ins)
                    tmp = tmpa if half == 0 else tmpb
                    W(sc, ts, qn)
                    G(sc, "phitmp%d" % half)
                    sc.activation(tmp[:, 0:512], PS[b][:, 0:512], AF.Relu,
                                  scale=-1.0)
                    ins = sc.activation(tmp[:, 0:512], tmp[:, 0:512], AF.Exp,
                                        scale=-1.0)
                    I(ss, ins)
                    W(ve, ss)
                    G(ve, "qf")
                    ins = ve.scalar_tensor_tensor(
                        qf[:, half * 512:half * 512 + 512], PS[b][:, 0:512],
                        0.0, tmp[:, 0:512], ALU.max, ALU.add)
                    I(vs, ins)
                    setg("P%d" % b, vs)
                    setg("phitmp%d" % half, vs)
                G(te, "PT")
                W(te, vs, kf_done)
                for j in range(8):
                    ins = te.transpose(PT[:, j * 128:(j + 1) * 128],
                                       kf3[:, j, :], ident)
                ktpn = I(ts, ins)
                W(sc, ts, ktpn)
                G(sc, "ktm")
                sc.copy(ktm[:, 0:512], PT[:, 0:512])
                ins = sc.copy(ktm[:, 512:1024], PT[:, 512:1024])
                ktm_ev = I(ss, ins)
                setg("PT", ss)
                for half in range(2):
                    b = (5, 6)[half]
                    G(te, "P%d" % b)
                    for k in range(8):
                        ins = te.matmul(PS[b][:, 0:512], lf3[:, k, :],
                                        wv3[:, k, half * 512:half * 512 + 512],
                                        start=(k == 0), stop=(k == 7))
                    vn = I(ts, ins)
                    W(sc, ts, vn)
                    G(sc, "vtm")
                    ins = sc.activation(vtm[:, half * 512:half * 512 + 512],
                                        PS[b][:, 0:512], AF.Copy)
                    I(ss, ins)
                    setg("P%d" % b, ss)
                setg("lnf%d" % (c % 2), ts)
                return ktm_ev

            def emit_heads(c, ktm_dn):
                n_in_seq = c % 16
                if n_in_seq == 0:
                    G(ve, "Sstate")
                    ve.memset(Sf, 0.0)
                    ve.memset(Zf, 0.0)
                    ve.memset(Sb, 0.0)
                    ins = ve.memset(Zb, 0.0)
                    I(vs, ins)
                W(te, ss, ktm_dn)
                yv = ytm[:, (c % 2) * DM:(c % 2) * DM + DM]
                atb_v = {}

                def emit_at(h):
                    hh, po = h // 2, (h % 2) * 64
                    pA = 4 + (h % 2)
                    G(te, "P%d" % pA)
                    ins = te.matmul(PS[pA][:, 0:128], kf3[po:po + 64, hh, :],
                                    qf3[po:po + 64, hh, :], start=True,
                                    stop=True)
                    atn = I(ts, ins)
                    W(ve, ts, atn)
                    ins = ve.tensor_tensor(atb3[:, h, :], PS[pA][:, 0:128],
                                           maskut, ALU.mult)
                    atb_v[h] = I(vs, ins)
                    setg("P%d" % pA, vs)

                emit_at(0)
                for h in range(16):
                    hh, po = h // 2, (h % 2) * 64
                    pKV = 0 + (h % 2)
                    pN = 6 if h % 2 == 0 else 2
                    if h + 1 < 16:
                        emit_at(h + 1)
                    G(te, "P%d" % pKV)
                    te.matmul(PS[pKV][po:po + 64, 0:64],
                              ktm[:, h * 64:h * 64 + 64],
                              vtm[:, h * 64:h * 64 + 64], start=True,
                              stop=True)
                    te.matmul(PS[pKV][po:po + 64, 64:65],
                              ktm[:, h * 64:h * 64 + 64], onesc[:, 0:1],
                              start=True, stop=True)
                    G(te, "P%d" % pN)
                    W(te, vs, atb_v[h])
                    te.matmul(PS[pN][:, 0:64], atb3[:, h, :],
                              vtm[:, h * 64:h * 64 + 64], start=True,
                              stop=False)
                    te.matmul(PS[pN][:, 0:64], qf3[po:po + 64, hh, :],
                              Sb[po:po + 64, hh * 64:hh * 64 + 64],
                              start=False, stop=True)
                    te.matmul(PS[pN][:, 64:65], atb3[:, h, :], onesc[:, 0:1],
                              start=True, stop=False)
                    ins = te.matmul(PS[pN][:, 64:65], qf3[po:po + 64, hh, :],
                                    Zb[po:po + 64, hh:hh + 1],
                                    start=False, stop=True)
                    hdn = I(ts, ins)
                    s4 = st[1][:, (h % 2):(h % 2) + 1]
                    s6 = st[1][:, 2 + (h % 2):3 + (h % 2)]
                    W(sc, ts, hdn)
                    G(sc, "den%d" % (h % 2))
                    ins = sc.activation(s4, PS[pN][:, 64:65], AF.Identity,
                                        bias=ceps6)
                    I(ss, ins)
                    W(ve, ss)
                    ins = ve.reciprocal(s6, s4)
                    I(vs, ins)
                    W(sc, vs)
                    if h == 0:
                        G(sc, "ytm%d" % (c % 2))
                    ins = sc.activation(yv[:, h * 64:h * 64 + 64],
                                        PS[pN][:, 0:64], AF.Identity,
                                        scale=s6)
                    I(ss, ins)
                    setg("P%d" % pN, ss)
                    setg("den%d" % (h % 2), ss)
                    ve.tensor_tensor(Sf[po:po + 64, hh * 64:hh * 64 + 64],
                                     Sf[po:po + 64, hh * 64:hh * 64 + 64],
                                     PS[pKV][po:po + 64, 0:64], ALU.add)
                    ve.tensor_tensor(Zf[po:po + 64, hh:hh + 1],
                                     Zf[po:po + 64, hh:hh + 1],
                                     PS[pKV][po:po + 64, 64:65], ALU.add)
                    ve.tensor_copy(Sb[po:po + 64, hh * 64:hh * 64 + 64],
                                   Sf[po:po + 64, hh * 64:hh * 64 + 64])
                    ins = ve.tensor_copy(Zb[po:po + 64, hh:hh + 1],
                                         Zf[po:po + 64, hh:hh + 1])
                    I(vs, ins)
                    setg("P%d" % pKV, vs)
                setg("Sstate", ts)
                setg("ktm", ts)
                setg("vtm", ts)
                setg("qf", ts)
                setg("kf", ts)
                y_ev[c] = ss.n
                if c + 1 < NTILE:
                    emit_lnf_tp(c + 1)

            def emit_proj_mm(c):
                yv_ = ytm[:, (c % 2) * DM:(c % 2) * DM + DM]
                G(te, "PT")
                W(te, ss, y_ev[c])
                for j in range(8):
                    ins = te.transpose(PT[:, j * 128:(j + 1) * 128],
                                       yv_[:, j * 128:(j + 1) * 128], ident)
                ytpn = I(ts, ins)
                setg("ytm%d" % (c % 2), ts)
                W(sc, ts, ytpn)
                G(sc, "yfm")
                sc.copy(yfm[:, 0:512], PT[:, 0:512])
                ins = sc.copy(yfm[:, 512:1024], PT[:, 512:1024])
                yfm_ev = I(ss, ins)
                setg("PT", ss)
                W(te, ss, yfm_ev)
                pj = []
                for half in range(2):
                    G(te, "P3")
                    for k in range(8):
                        ins = te.matmul(PS[3][:, 0:512], yfm3[:, k, :],
                                        wp3[:, k, half * 512:half * 512 + 512],
                                        start=(k == 0), stop=(k == 7))
                    pj.append(I(ts, ins))
                    W(ve, ts, pj[-1])
                    G(ve, "psb")
                    ins = ve.scalar_tensor_tensor(
                        psb[:, half * 512:half * 512 + 512], PS[3][:, 0:512],
                        1.0, bpbc[:, half * 512:half * 512 + 512], ALU.mult,
                        ALU.add, accum_out=st[2][:, half:half + 1])
                    I(vs, ins)
                    setg("P3", vs)
                setg("yfm", ts)

            def emit_proj_tail(c):
                ve.scalar_tensor_tensor(junk[:, 0:512], psb[:, 0:512], 1.0,
                                        psb[:, 0:512], ALU.mult, ALU.mult,
                                        accum_out=st[2][:, 2:3])
                ins = ve.scalar_tensor_tensor(junk[:, 512:1024],
                                              psb[:, 512:1024],
                                              psb[:, 512:1024], None,
                                              ALU.bypass, ALU.mult) \
                    if False else ve.scalar_tensor_tensor(
                        junk[:, 512:1024], psb[:, 512:1024], 1.0,
                        psb[:, 512:1024], ALU.mult, ALU.mult,
                        accum_out=st[2][:, 3:4])
                I(vs, ins)
                W(sc, vs)
                sc.activation(st[2][:, 0:1], st[2][:, 0:1], AF.Identity,
                              bias=st[2][:, 1:2])
                sc.activation(st[2][:, 2:3], st[2][:, 2:3], AF.Identity,
                              bias=st[2][:, 3:4])
                ins = sc_ln_stats(st[2][:, 0:1], st[2][:, 2:3],
                                  st[2][:, 4:5], st[2][:, 5:6],
                                  st[2][:, 6:7], st[2][:, 7:8])
                I(ss, ins)
                W(ve, ss)
                xv = x_view(c)
                xpv = xpb[:, (c % 2) * DM:(c % 2) * DM + DM]
                G(ve, "xp%d" % (c % 2))
                ins = ve.tensor_scalar(tmpc, psb, st[2][:, 4:5],
                                       st[2][:, 5:6], ALU.subtract, ALU.mult)
                FV(ins)
                ins = ve.tensor_tensor(xpv, tmpc, xv, ALU.add)
                FV(ins)
                setg("xb%d" % (c % 4), vs)
                setg("psb", vs)
                ve.tensor_reduce(st[3][:, 0:1], xpv, AX.X, ALU.add)
                ins = ve.scalar_tensor_tensor(junk, xpv, 1.0, xpv, ALU.mult,
                                              ALU.mult,
                                              accum_out=st[3][:, 1:2])
                I(vs, ins)
                W(sc, vs)
                ins = sc_ln_stats(st[3][:, 0:1], st[3][:, 1:2],
                                  st[3][:, 2:3], st[3][:, 3:4],
                                  st[3][:, 4:5], st[3][:, 5:6])
                I(ss, ins)
                W(ve, ss)
                ln3v = ln3b[:, (c % 2) * DM:(c % 2) * DM + DM]
                G(ve, "l3%d" % (c % 2))
                ins = ve.tensor_scalar(ln3v, xpv, st[3][:, 2:3],
                                       st[3][:, 3:4], ALU.subtract, ALU.mult)
                I(vs, ins)
                W(sy, vs)
                db(sy, sds)
                ins = sy.dma_start(out=ap["xprime"][c * 128:(c + 1) * 128, :],
                                   in_=xpv)
                I(sds, ins, 16)
                if not phase_b:
                    ins = sy.dma_start(out=ap["out"][c * 128:(c + 1) * 128, :],
                                       in_=xpv)
                    I(sds, ins, 16)
                setg("xp%d" % (c % 2), sds)
                ln3_done[c] = vs.n

            def emit_gate(c):
                ln3v = ln3b[:, (c % 2) * DM:(c % 2) * DM + DM]
                G(te, "PT")
                W(te, vs, ln3_done[c])
                for j in range(8):
                    ins = te.transpose(PT[:, j * 128:(j + 1) * 128],
                                       ln3v[:, j * 128:(j + 1) * 128], ident)
                gtpn = I(ts, ins)
                setg("l3%d" % (c % 2), ts)
                W(sc, ts, gtpn)
                G(sc, "g_lnf")
                G(sc, "g_lnfs")
                sc.copy(g_lnf[:, 0:512], PT[:, 0:512])
                ins = sc.copy(g_lnf[:, 512:1024], PT[:, 512:1024])
                g_ev = I(ss, ins)
                setg("PT", ss)
                W(sy, ss, g_ev)
                db(sy, sds)
                for k in range(8):
                    ins = sy.dma_start(
                        out=ap["ln3s"][k * 128:(k + 1) * 128,
                                       c * 128:(c + 1) * 128],
                        in_=g_lnf3[:, k, :])
                    I(sds, ins, 16)
                setg("g_lnfs", sds)
                W(te, ss, g_ev)
                G(te, "P3")
                for k in range(8):
                    ins = te.matmul(PS[3][:, 0:NEXP], g_lnf3[:, k, :],
                                    gw3[:, k, :], start=(k == 0), stop=(k == 7))
                gmm = I(ts, ins)
                setg("g_lnf", ts)
                W(ve, ts, gmm)
                ve.tensor_copy(glog, PS[3][:, 0:NEXP])
                ins = ve.reduce_max(st[4][:, 0:1], PS[3][:, 0:NEXP], AX.X)
                I(vs, ins)
                setg("P3", vs)
                W(sc, vs)
                sc.mul(st[4][:, 1:2], st[4][:, 0:1], -1.0)
                ins = sc.activation(st[5][:, 0:NEXP], glog, AF.Exp,
                                    bias=st[4][:, 1:2],
                                    accum_out=st[4][:, 2:3])
                I(ss, ins)
                W(ve, ss)
                ve.reciprocal(st[4][:, 3:4], st[4][:, 2:3])
                ins = ve.tensor_scalar(st[6][:, 0:NEXP], glog, st[4][:, 0:1],
                                       None, ALU.is_equal)
                FV(ins)
                ve.scalar_tensor_tensor(st[7][:, 0:NEXP], st[6][:, 0:NEXP],
                                        -1e30, glog, ALU.mult, ALU.add)
                ve.tensor_scalar(st[5][:, 0:NEXP], st[5][:, 0:NEXP],
                                 st[4][:, 3:4], None, ALU.mult)
                ins = ve.reduce_max(st[4][:, 4:5], st[7][:, 0:NEXP], AX.X)
                FV(ins)
                ins = ve.tensor_scalar(st[7][:, 0:NEXP], st[7][:, 0:NEXP],
                                       st[4][:, 4:5], None, ALU.is_equal)
                FV(ins)
                ins = ve.tensor_tensor(st[6][:, 0:NEXP], st[6][:, 0:NEXP],
                                       st[7][:, 0:NEXP], ALU.add)
                FV(ins)
                ins = ve.tensor_tensor(gate[:, c * NEXP:(c + 1) * NEXP],
                                       st[5][:, 0:NEXP], st[6][:, 0:NEXP],
                                       ALU.mult)
                I(vs, ins)

            emit_pre(0)
            emit_lnf_tp(0)
            for c in range(NTILE):
                if c + 1 < NTILE:
                    emit_pre(c + 1)
                kdn = emit_qkv(c)
                if c >= 2:
                    emit_gate(c - 2)
                if c >= 1:
                    emit_proj_mm(c - 1)
                emit_heads(c, kdn)
                if c >= 1:
                    emit_proj_tail(c - 1)
            emit_gate(NTILE - 2)
            emit_proj_mm(NTILE - 1)
            emit_proj_tail(NTILE - 1)
            emit_gate(NTILE - 1)

            a_end = {k: sm[k].n for k in sem_names}

            if not phase_b:
                W(sy, sds)
                return

            for eng in (gp, sc, te, ve, sy):
                for k in sem_names:
                    W(eng, sm[k], a_end[k])

            B = Arena(ab_t, AB_N)
            Fb = Arena(af_t, AF_N)
            xtf = B.take(8 * TCB)
            xtf3 = xtf.rearrange("p (k n) -> p k n", k=8)
            slot = B.take(2 * ESLOT)
            hb = B.take(8 * TCB)
            hb3 = hb.rearrange("p (m n) -> p m n", m=8)
            hq = B.take(8 * 512)
            hq3 = hq.rearrange("p (m n) -> p m n", m=8)
            junkb = B.take(SI + 1)

            yb = Fb.take(NTB * DM)
            yb3 = yb.rearrange("p (t f) -> p t f", t=NTB)
            asb = Fb.take(8 * 512)
            asb3 = asb.rearrange("p (m n) -> p m n", m=8)
            xpl = Fb.take(2 * DM)
            outb = Fb.take(2 * DM)
            junk2 = Fb.take(DM)
            junk3 = Fb.take(DM)
            hcp = Fb.take(2 * DM)

            btw5x = sb["btw5x"][:, :].rearrange("p (s j n) -> p s j n", s=2, j=4)
            bt5 = sb["bt5"][:, :]
            be1 = sb["be1"][:, :]
            be3 = sb["be3"][:, :]
            bs1 = sb["bs1"][:, :]
            bs3 = sb["bs3"][:, :]
            v2c = sb["v2c"][:, :]
            dotc = sb["dotc"][:, :]
            lam = sb["lam"][:, :]
            lami = sb["lami"][:, :]
            uev = sb["ue"][:, :]
            timeb = sb["timeb"][:, :]
            onesf = sb["onesf"][:, :]
            gate_r = sb["gate"][:, :].rearrange("p (t e) -> p t e", t=NTILE)


            xt_pref = {}

            def emit_xt_load(bc):
                G(gp, "xtf")
                db(gp, sdt)
                for k in range(8):
                    ins = gp.dma_start(
                        out=xtf3[:, k, :],
                        in_=ap["ln3s"][k * 128:(k + 1) * 128,
                                       bc * TCB:(bc + 1) * TCB])
                    I(sdt, ins, 16)
                xt_pref[bc] = sdt.n

            emit_xt_load(0)
            pending_v = []
            for bc in range(NBCHUNK):
                G(ve, "hb3")
                G(ve, "hb7")
                ve.memset(hb3[:, 3, :], 0.0)
                ins = ve.memset(hb3[:, 7, :], 0.0)
                I(vs, ins)
                xt_done = xt_pref[bc]

                ve.memset(yb, 0.0)
                ve.memset(lami, 1.0)
                ve.memset(yb3[:, :, 0:1], 1.0)
                ins = ve.memset(lam, 1.0)
                I(vs, ins)

                for t in range(2):
                    G(sy, "xpl%d" % (t % 2))
                    db(sy, sdx)
                    ins = sy.dma_start(
                        out=xpl[:, (t % 2) * DM:(t % 2) * DM + DM],
                        in_=ap["xprime"][(bc * NTB + t) * 128:
                                         (bc * NTB + t + 1) * 128, :])
                    I(sdx, ins, 16)
                xpl_n = {0: sdx.n - 16, 1: sdx.n}

                for e in range(NEXP + 1):
                    shared = (e == NEXP)
                    nm = 8 if shared else 4
                    k2 = 8 if shared else 4
                    if shared:
                        w1v = slot[:, 0:8 * SI].rearrange(
                            "p (k m) -> p k m", k=8)
                        w3v = slot[:, 8 * SI:16 * SI].rearrange(
                            "p (k m) -> p k m", k=8)
                        w2v = slot[:, 16 * SI:24 * SI].rearrange(
                            "p (k m) -> p k m", k=8)
                    else:
                        base = (e % 2) * ESLOT
                        w1v = slot[:, base:base + 8 * EI].rearrange(
                            "p (k m) -> p k m", k=8)
                        w3v = slot[:, base + 8 * EI:base + 16 * EI].rearrange(
                            "p (k m) -> p k m", k=8)
                        w2v = slot[:, base + 16 * EI:base + 16 * EI + 4 * SI
                                   ].rearrange("p (k m) -> p k m", k=4)

                    if shared:
                        for kk in ("slotA0", "slotA1", "slotB0", "slotB1",
                                   "btw5x0", "btw5x1"):
                            G(gp, kk)
                        db(gp, gd)
                        for wv_, src in ((w1v, "ws1_t"), (w3v, "ws3_t")):
                            ins = gp.dma_start(out=wv_, in_=ap[src].rearrange(
                                "(k p) m -> p k m", p=128))
                            I(gd, ins, 16)
                        w13_rdy = gd.n
                        db(gp, gd)
                        ins = gp.dma_start(out=w2v, in_=ap["ws2p_t"].rearrange(
                            "(k p) m -> p k m", p=128))
                        I(gd, ins, 16)
                        db(gp, gd)
                        ins = gp.dma_start(
                            out=btw5x[:, 0, :, :],
                            in_=ap["btw5s"].rearrange("j p n -> p j n"))
                        I(gd, ins, 16)
                    else:
                        G(gp, "slotA%d" % (e % 2))
                        db(gp, gd)
                        for wv_, src in ((w1v, "we1_t"), (w3v, "we3_t")):
                            ins = gp.dma_start(
                                out=wv_,
                                in_=ap[src][e].rearrange("(k p) m -> p k m",
                                                         p=128))
                            I(gd, ins, 16)
                        w13_rdy = gd.n
                        G(gp, "slotB%d" % (e % 2))
                        db(gp, gd)
                        ins = gp.dma_start(
                            out=w2v,
                            in_=ap["we2p_t"][e].rearrange("(k p) m -> p k m",
                                                          p=128))
                        I(gd, ins, 16)
                        G(gp, "btw5x%d" % (e % 2))
                        db(gp, gd)
                        ins = gp.dma_start(
                            out=btw5x[:, e % 2, :, :],
                            in_=ap["btw5e"][e].rearrange("j p n -> p j n"))
                        I(gd, ins, 16)
                    w_rdy = gd.n

                    def uevblk(shared=shared, e=e, bc=bc):
                        if shared:
                            ins = ve.tensor_copy(uev, lami)
                        else:
                            ins = ve.tensor_tensor(
                                uev,
                                gate_r[:, bc * NTB:(bc + 1) * NTB, e:e + 1
                                       ].rearrange("p t o -> p (t o)"),
                                lami, ALU.mult)
                        I(vs, ins)
                    pending_v.append(uevblk)

                    W(te, gd, w13_rdy)
                    W(te, sdt, xt_done)
                    sq_f = [0, 0]
                    hvn_m = {}
                    for f in range(2):
                        for m in range(nm):
                            rows = 128 if m < nm - 1 else 127
                            mt = f * nm + m
                            bB = (0, 1, 4)[mt % 3]
                            bK = "P4s" if bB == 4 else "P%d" % bB
                            aB = (2, 3)[mt % 2]
                            G(te, bK)
                            for k in range(8):
                                ins = te.matmul(
                                    PS[bB][0:rows, 0:512],
                                    w3v[:, k, m * 128:m * 128 + rows],
                                    xtf3[:, k, f * 512:f * 512 + 512],
                                    start=(k == 0), stop=(k == 7))
                            bmm = I(ts, ins)
                            G(te, "P%d" % aB)
                            for k in range(8):
                                ins = te.matmul(
                                    PS[aB][0:rows, 0:512],
                                    w1v[:, k, m * 128:m * 128 + rows],
                                    xtf3[:, k, f * 512:f * 512 + 512],
                                    start=(k == 0), stop=(k == 7))
                            amm = I(ts, ins)
                            if f == 1 and m == nm - 1:
                                setg("slotA%d" % (0 if shared else e % 2), ts)
                                if shared:
                                    setg("slotA1", ts)
                                    setg("xtf", ts)
                            bcol = (bs1[:, m:m + 1] if shared
                                    else be1[:, e * 4 + m:e * 4 + m + 1])
                            W(sc, ts, amm)
                            G(sc, "asb%d" % m)
                            ins = sc.activation(asb3[0:rows, m, :],
                                                PS[aB][0:rows, 0:512],
                                                AF.Silu, bias=bcol[0:rows, :])
                            I(ss, ins)
                            setg("P%d" % aB, ss)
                            b3col = (bs3[:, m:m + 1] if shared
                                     else be3[:, e * 4 + m:e * 4 + m + 1])
                            W(ve, ss)
                            W(ve, ts, bmm)
                            G(ve, "hb%d" % m)
                            ins = ve.scalar_tensor_tensor(
                                hb3[0:rows, m, f * 512:f * 512 + 512],
                                PS[bB][0:rows, 0:512], b3col[0:rows, :],
                                asb3[0:rows, m, :], ALU.add, ALU.mult)
                            hvn = I(vs, ins)
                            hvn_m[(f, m)] = hvn
                            setg(bK, vs)
                            setg("asb%d" % m, vs)
                            if f == 0:
                                W(sc, vs, hvn)
                                G(sc, "hq%d" % m)
                                ins = sc.activation(
                                    hq3[:, m, :],
                                    hb3[:, m, 0:512], AF.Square)
                                I(ss, ins)
                            if pending_v:
                                pending_v.pop(0)()
                        if f == 0:
                            hq0_ss = ss.n
                    b2r5 = btw5x[:, 0 if shared else e % 2, :, :]

                    def emit_gg_h(gg):
                        p0 = 5 if gg % 2 == 0 else 3
                        p1 = 6 if gg % 2 == 0 else 2
                        G(te, "P%d" % p0)
                        G(te, "P%d" % p1)
                        for half, (n0, nw, pb) in enumerate(
                                [(0, 512, p0), (512, 511, p1)]):
                            for k in range(k2):
                                te.matmul(
                                    PS[pb][:, 0:nw],
                                    hb3[:, k, gg * 128:(gg + 1) * 128],
                                    w2v[:, k, n0:n0 + nw],
                                    start=(k == 0), stop=False)

                    def emit_gg(gg):
                        p0 = 5 if gg % 2 == 0 else 3
                        p1 = 6 if gg % 2 == 0 else 2
                        W(te, ss, tr0 if gg < 4 else tr1)
                        hcol = 0 if gg < 4 else 128
                        for half, (n0, nw, pb) in enumerate(
                                [(0, 512, p0), (512, 511, p1)]):
                            ins2 = te.matmul(PS[pb][:, 0:nw],
                                             bt5[0:5, hcol:hcol + 128],
                                             b2r5[0:5, gg % 4, n0:n0 + nw],
                                             start=False, stop=True)
                        w2n = I(ts, ins2)
                        hv = hcp[:, (gg % 2) * DM:(gg % 2) * DM + SI]
                        W(sc, ts, w2n)
                        G(sc, "hcp%d" % (gg % 2))
                        sc.copy(hv[:, 0:512], PS[p0][:, 0:512])
                        ins2 = sc.copy(hv[:, 512:1023], PS[p1][:, 0:511])
                        I(ss, ins2)
                        setg("P%d" % p0, ss)
                        setg("P%d" % p1, ss)
                        ins2 = sc.activation(junkb[:, 0:SI], hv[:, 0:SI],
                                             AF.Square,
                                             accum_out=v2c[:, gg:gg + 1])
                        ssn2 = I(ss, ins2)

                        def vblock(gg=gg, hv=hv, ssn2=ssn2):
                            W(ve, ss, ssn2)
                            ve.scalar_tensor_tensor(
                                junk2[:, 0:512], hv[:, 0:512], 1.0,
                                yb3[:, gg, 1:513], ALU.mult, ALU.mult,
                                accum_out=st[6][:, 0:1])
                            ve.scalar_tensor_tensor(
                                junk3[:, 0:511], hv[:, 512:1023], 1.0,
                                yb3[:, gg, 513:1024], ALU.mult, ALU.mult,
                                accum_out=st[7][:, 0:1])
                            ve.scalar_tensor_tensor(
                                yb3[:, gg, 1:513], hv[:, 0:512],
                                uev[:, gg:gg + 1], yb3[:, gg, 1:513],
                                ALU.mult, ALU.add)
                            ve.scalar_tensor_tensor(
                                yb3[:, gg, 513:1024], hv[:, 512:1023],
                                uev[:, gg:gg + 1], yb3[:, gg, 513:1024],
                                ALU.mult, ALU.add)
                            ins3 = ve.tensor_tensor(dotc[:, gg:gg + 1],
                                                    st[6][:, 0:1],
                                                    st[7][:, 0:1],
                                                    ALU.add)
                            I(vs, ins3)
                            setg("hcp%d" % (gg % 2), vs)

                        if gg >= 6:
                            pending_v.append(vblock)
                        else:
                            vblock()

                    W(te, ss, hq0_ss)
                    G(te, "P4s")
                    for gg in range(4):
                        for m in range(nm):
                            cc = 1 if m == nm - 1 else 0
                            ins = te.matmul(
                                PS[4][:, gg:gg + 1],
                                hq3[:, m, gg * 128:(gg + 1) * 128],
                                onesc[:, cc:cc + 1],
                                start=(m == 0), stop=(m == nm - 1))
                    ssq0 = I(ts, ins)
                    W(te, gd, w_rdy)
                    emit_gg_h(0)
                    W(sc, ts, ssq0)
                    G(sc, "timeb")
                    ins = sc.activation(timeb[:, 0:4], PS[4][:, 0:4],
                                        AF.Sqrt, bias=cone)
                    sq_f[0] = I(ss, ins)
                    W(te, ss, sq_f[0])
                    G(te, "PT")
                    ins = te.transpose(PT[0:4, 0:128], timeb[:, 0:4], ident)
                    ttn0 = I(ts, ins)
                    W(sc, ts, ttn0)
                    G(sc, "bt5h0")
                    ins = sc.copy(bt5[0:4, 0:128], PT[0:4, 0:128])
                    tr0 = I(ss, ins)
                    setg("PT", ss)
                    hq1_ss = 0
                    for m in range(nm):
                        W(sc, vs, hvn_m[(1, m)])
                        G(sc, "hq%d" % m)
                        ins = sc.activation(hq3[:, m, :],
                                            hb3[:, m, 512:1024], AF.Square)
                        hq1_ss = I(ss, ins)

                    emit_gg(0)
                    emit_gg_h(1)
                    W(te, ss, max(sq_f[0], hq1_ss))
                    for gg in range(4):
                        for m in range(nm):
                            cc = 1 if m == nm - 1 else 0
                            ins = te.matmul(
                                PS[4][:, 4 + gg:5 + gg],
                                hq3[:, m, gg * 128:(gg + 1) * 128],
                                onesc[:, cc:cc + 1],
                                start=(m == 0), stop=(m == nm - 1))
                    ssq1 = I(ts, ins)
                    W(sc, ts, ssq1)
                    ins = sc.activation(timeb[:, 4:8], PS[4][:, 4:8],
                                        AF.Sqrt, bias=cone)
                    sq_f[1] = I(ss, ins)
                    setg("P4s", ss)
                    emit_gg(1)
                    emit_gg_h(2)
                    W(te, ss, sq_f[1])
                    G(te, "PT")
                    ins = te.transpose(PT[0:4, 0:128], timeb[:, 4:8], ident)
                    ttn1 = I(ts, ins)
                    setg("timeb", ts)
                    W(sc, ts, ttn1)
                    G(sc, "bt5h1")
                    ins = sc.copy(bt5[0:4, 128:256], PT[0:4, 0:128])
                    tr1 = I(ss, ins)
                    setg("PT", ss)
                    emit_gg(2)
                    emit_gg_h(3)
                    emit_gg(3)
                    for gg in range(4, NTB):
                        emit_gg_h(gg)
                        emit_gg(gg)
                    for m in range(nm):
                        setg("hb%d" % m, ts)
                    setg("hb3", ts)
                    setg("hb7", ts)
                    setg("slotB%d" % (0 if shared else e % 2), ts)
                    if shared:
                        setg("slotB1", ts)
                    setg("bt5h0", ts)
                    setg("bt5h1", ts)
                    setg("btw5x%d" % (e % 2), ts)

                    W(sc, ss)
                    ins = sc.activation(st[8][:, 0:8], v2c[:, 0:8], AF.Sqrt,
                                        bias=cone)
                    t2n = I(ss, ins)

                    def s2blk(t2n=t2n, shared=shared, e=e, bc=bc):
                        W(ve, ss, t2n)
                        y0v = yb3[:, :, 0:1].rearrange("p t o -> p (t o)")
                        wvv = onesf if shared else gate_r[
                            :, bc * NTB:(bc + 1) * NTB, e:e + 1].rearrange(
                            "p t o -> p (t o)")
                        ve.tensor_tensor(st[9][:, 0:8], y0v, st[8][:, 0:8],
                                         ALU.mult)
                        ve.tensor_tensor(st[10][:, 0:8], wvv, wvv, ALU.mult)
                        ve.tensor_tensor(st[9][:, 0:8], dotc[:, 0:8],
                                         st[9][:, 0:8], ALU.subtract)
                        ve.tensor_scalar(st[10][:, 0:8], st[10][:, 0:8], 1.0,
                                         None, ALU.add)
                        ve.tensor_tensor(st[9][:, 0:8], lam, st[9][:, 0:8],
                                         ALU.mult)
                        ve.tensor_tensor(st[11][:, 0:8], uev, st[8][:, 0:8],
                                         ALU.mult)
                        ve.tensor_tensor(st[9][:, 0:8], wvv, st[9][:, 0:8],
                                         ALU.mult)
                        ins = ve.tensor_tensor(y0v, y0v, st[11][:, 0:8],
                                               ALU.add)
                        FV(ins)
                        ins = ve.scalar_tensor_tensor(st[12][:, 0:8],
                                                      st[9][:, 0:8], -2.0,
                                                      st[10][:, 0:8],
                                                      ALU.mult, ALU.add)
                        I(vs, ins)
                        W(sc, vs)
                        sc.activation(st[13][:, 0:8], st[12][:, 0:8], AF.Relu,
                                      bias=cneg6)
                        ins = sc.activation(st[13][:, 0:8], st[13][:, 0:8],
                                            AF.Sqrt, bias=ceps6)
                        sqn = I(ss, ins)
                        W(ve, ss, sqn)
                        ve.reciprocal(st[14][:, 0:8], st[13][:, 0:8])
                        ve.tensor_tensor(lami, lami, st[13][:, 0:8], ALU.mult)
                        ins = ve.tensor_tensor(lam, lam, st[14][:, 0:8],
                                               ALU.mult)
                        I(vs, ins)
                    pending_v.append(s2blk)

                if False:
                    W(sy, vs)
                    W(sy, ss)
                    db(sy, sds)
                    I(sds, sy.dma_start(out=ap["db_y"], in_=yb), 16)
                    db(gp, gd)
                    I(gd, gp.dma_start(out=ap["db_xtf"], in_=xtf), 16)
                    I(gd, gp.dma_start(out=ap["db_h"], in_=hb), 16)
                    W(sy, gd)
                    for ii, bufv in enumerate((v2c, dotc, lam, lami, uev,
                                               st[8][:, 0:8])):
                        I(sds, sy.dma_start(
                            out=ap["db_st"][:, ii * 8:(ii + 1) * 8],
                            in_=bufv[:, 0:8]), 16)

                while pending_v:
                    pending_v.pop(0)()
                if bc + 1 < NBCHUNK:
                    emit_xt_load(bc + 1)
                W(ve, vs)
                for t in range(NTB):
                    ve.scalar_tensor_tensor(junk2, yb3[:, t, :], 1.0,
                                            yb3[:, t, :], ALU.mult, ALU.mult,
                                            accum_out=st[1][:, t:t + 1])
                    ins = ve.tensor_reduce(st[0][:, t:t + 1], yb3[:, t, :],
                                           AX.X, ALU.add)
                I(vs, ins)
                W(sc, vs)
                lastin = None
                for t in range(NTB):
                    sc.mul(st[2][:, t:t + 1], st[0][:, t:t + 1], 1.0 / DM)
                    sc.mul(st[3][:, t:t + 1], st[1][:, t:t + 1], 1.0 / DM)
                    sc.square(st[4][:, t:t + 1], st[2][:, t:t + 1])
                    sc.activation(st[4][:, t:t + 1], st[4][:, t:t + 1],
                                  AF.Identity, scale=-1.0, bias=ceps5)
                    lastin = sc.activation(st[5][:, t:t + 1],
                                           st[3][:, t:t + 1], AF.Sqrt,
                                           bias=st[4][:, t:t + 1])
                fsn = I(ss, lastin)
                W(ve, ss, fsn)
                ins = ve.reciprocal(st[6][:, 0:8], st[5][:, 0:8])
                FV(ins)
                for t in range(NTB):
                    W(ve, sdx, xpl_n[t % 2])
                    G(ve, "outb%d" % (t % 2))
                    jk = junk2 if t % 2 == 0 else junk3
                    ins = ve.tensor_scalar(jk, yb3[:, t, :], st[2][:, t:t + 1],
                                           st[6][:, t:t + 1], ALU.subtract,
                                           ALU.mult)
                    FV(ins)
                    ins = ve.tensor_tensor(
                        outb[:, (t % 2) * DM:(t % 2) * DM + DM], jk,
                        xpl[:, (t % 2) * DM:(t % 2) * DM + DM], ALU.add)
                    I(vs, ins)
                    setg("xpl%d" % (t % 2), vs)
                    W(sy, vs)
                    if t + 2 < NTB:
                        db(sy, sdx)
                        ins = sy.dma_start(
                            out=xpl[:, ((t + 2) % 2) * DM:
                                    ((t + 2) % 2) * DM + DM],
                            in_=ap["xprime"][(bc * NTB + t + 2) * 128:
                                             (bc * NTB + t + 3) * 128, :])
                        xpl_n[(t + 2) % 2] = I(sdx, ins, 16)
                    db(sy, sds)
                    ins = sy.dma_start(
                        out=ap["out"][(bc * NTB + t) * 128:
                                      (bc * NTB + t + 1) * 128, :],
                        in_=outb[:, (t % 2) * DM:(t % 2) * DM + DM])
                    I(sds, ins, 16)
                    setg("outb%d" % (t % 2), sds)

            W(sy, sds)

        engs = {}

        @block.gpsimd
        def _(g):
            engs["gpsimd"] = g
            script("gpsimd", engs)

        @block.scalar
        def _(s):
            engs["scalar"] = s
            script("scalar", engs)

        @block.tensor
        def _(t):
            engs["tensor"] = t
            script("tensor", engs)

        @block.vector
        def _(v):
            engs["vector"] = v
            script("vector", engs)

        @block.sync
        def _(s):
            engs["sync"] = s
            script("sync", engs)

    return nc



def _prep_inputs(inp):
    f = np.float32
    asc = np.ascontiguousarray

    def T(a):
        return asc(np.asarray(a, f).T)

    We1 = np.asarray(inp["L0_ffn_We1"], f)
    We3 = np.asarray(inp["L0_ffn_We3"], f)
    We2 = np.asarray(inp["L0_ffn_We2"], f)
    we2p = np.transpose(We2, (0, 2, 1))
    we2p = asc(np.concatenate([we2p[:, 1:, :], we2p[:, 0:1, :]], axis=1))
    Ws2 = np.asarray(inp["L0_ffn_Ws2"], f)
    ws2p = Ws2.T
    ws2p = asc(np.concatenate([ws2p[1:, :], ws2p[0:1, :]], axis=0))

    btw5e = np.zeros((NEXP, 4, 5, SI), f)
    for j in range(4):
        btw5e[:, j, j, :] = we2p[:, EI, :]
        btw5e[:, j, 4, :] = np.asarray(inp["L0_ffn_be2"], f)
    btw5e = asc(btw5e)
    btw5s = np.zeros((4, 5, SI), f)
    for j in range(4):
        btw5s[j, j, :] = ws2p[SI, :]
        btw5s[j, 4, :] = np.asarray(inp["L0_ffn_bs2"], f)
    btw5s = asc(btw5s)
    bt5i = np.zeros((5, 256), np.float32)
    bt5i[4, :] = 1.0
    bt5i = bt5i.astype(ml_dtypes.bfloat16)

    def lay_be(b):
        ne, it = b.shape
        nmt = (it + 127) // 128
        bp = np.zeros((ne, nmt * 128), f)
        bp[:, :it] = b
        return asc(bp.reshape(ne, nmt, 128).transpose(2, 0, 1).reshape(
            128, ne * nmt))

    def lay_bs(b):
        it = b.shape[0]
        nmt = (it + 127) // 128
        bp = np.zeros(nmt * 128, f)
        bp[:it] = b
        return asc(bp.reshape(nmt, 128).T)

    shared = {
        "wq_t": T(inp["L0_attn_Wq"]),
        "wk_t": T(inp["L0_attn_Wk"]),
        "wv_t": T(inp["L0_attn_Wv"]),
        "wp_t": T(inp["L0_attn_Wp"]),
        "bp_bc": asc(np.broadcast_to(np.asarray(inp["L0_attn_bp"], f),
                                     (128, DM))),
        "gw_t": T(inp["L0_ffn_gate_w"]),
        "we1_t": asc(np.transpose(We1, (0, 2, 1))),
        "we3_t": asc(np.transpose(We3, (0, 2, 1))),
        "we2p_t": we2p,
        "be1_l": lay_be(np.asarray(inp["L0_ffn_be1"], f)),
        "be3_l": lay_be(np.asarray(inp["L0_ffn_be3"], f)),
        "btw5e": btw5e,
        "ws1_t": T(inp["L0_ffn_Ws1"]),
        "ws3_t": T(inp["L0_ffn_Ws3"]),
        "ws2p_t": ws2p,
        "bs1_l": lay_bs(np.asarray(inp["L0_ffn_bs1"], f)),
        "bs3_l": lay_bs(np.asarray(inp["L0_ffn_bs3"], f)),
        "btw5s": btw5s,
        "mask_ut": asc(np.triu(np.ones((128, 128), f))),
        "ident_bf": np.eye(128, dtype=ml_dtypes.bfloat16),
        "bt5_init": bt5i,
        "onescol": asc(np.stack([np.ones(128), np.concatenate(
            [np.ones(127), [0.0]])], axis=1).astype(ml_dtypes.bfloat16)),
    }
    x = np.asarray(inp["L0_attn_x"], f)
    in_maps = []
    for c in range(N_CORES):
        m = dict(shared)
        m["x"] = asc(x[2 * c:2 * c + 2].reshape(NTOK, DM))
        in_maps.append(m)
    return in_maps


def kernel(**inputs):
    in_maps = _prep_inputs(inputs)
    nc = build_program(phase_b=PHASE_B)
    res = run_bass_kernel_spmd(nc, in_maps, core_ids=list(range(N_CORES)))
    out = np.concatenate(
        [r["out"].reshape(2, 2048, DM) for r in res.results], axis=0)
    return out


# version 24 (final)
# speedup vs baseline: 1.0407x; 1.0407x over previous
import numpy as np
import ml_dtypes
from contextlib import ExitStack

import concourse.bass as bass
import concourse.mybir as mybir
from concourse.bass_utils import run_bass_kernel_spmd

F32 = mybir.dt.float32
BF16 = mybir.dt.bfloat16
AF = mybir.ActivationFunctionType
ALU = mybir.AluOpType
AX = mybir.AxisListType

N_CORES = 8
DM = 1024
NTOK = 4096
NTILE = 32
TCB = 1024
NTB = 8
NBCHUNK = 4
NEXP = 16
EI = 511
SI = 1023
LN_EPS = 1e-5
ATT_EPS = 1e-6
ESLOT = 12276

PHASE_B = True
_DBG_CHUNK = -1
_DBG_B = -1


class _Dummy:
    def then_inc(self, *a, **k):
        return self

_DUM = _Dummy()


class EngP:
    def __init__(self, eng, active):
        self._eng = eng
        self._act = active

    def __getattr__(self, name):
        if self._act:
            return getattr(self._eng, name)
        return lambda *a, **k: _DUM


class Sem:
    __slots__ = ("h", "n")

    def __init__(self, h):
        self.h = h
        self.n = 0


def I(sem, instr, k=1):
    sem.n += k
    if instr is not _DUM:
        instr.then_inc(sem.h, k)
    return sem.n


def W(eng, sem, val=None):
    v = sem.n if val is None else val
    if v > 0:
        eng.wait_ge(sem.h, v)


class Arena:
    def __init__(self, tensor, nelem):
        self.t = tensor
        self.off = 0
        self.n = nelem

    def take(self, elems, align=64):
        self.off = (self.off + align - 1) // align * align
        v = self.t[:, self.off:self.off + elems]
        self.off += elems
        assert self.off <= self.n, f"arena overflow {self.off} > {self.n}"
        return v


def build_program(phase_b=PHASE_B):
    nc = bass.Bass("TRN2", target_bir_lowering=False,
                   detect_race_conditions=False)
    ap = {}

    def din(name, shape, dtype=F32):
        ap[name] = nc.dram_tensor(name, shape, dtype, kind="ExternalInput").ap()

    din("x", [NTOK, DM])
    for w in ("wq_t", "wk_t", "wv_t", "wp_t"):
        din(w, [DM, DM])
    din("bp_bc", [128, DM])
    din("gw_t", [DM, NEXP])
    din("we1_t", [NEXP, DM, EI])
    din("we3_t", [NEXP, DM, EI])
    din("we2p_t", [NEXP, EI + 1, SI])
    din("be1_l", [128, NEXP * 4])
    din("be3_l", [128, NEXP * 4])
    din("btw5e", [NEXP, 4, 5, SI])
    din("ws1_t", [DM, SI])
    din("ws3_t", [DM, SI])
    din("ws2p_t", [DM, SI])
    din("bs1_l", [128, 8])
    din("bs3_l", [128, 8])
    din("btw5s", [4, 5, SI])
    din("bt5_init", [5, 256], BF16)
    din("mask_ut", [128, 128])
    din("ident_bf", [128, 128], BF16)
    din("onescol", [128, 2], BF16)

    ap["out"] = nc.dram_tensor("out", [NTOK, DM], F32,
                               kind="ExternalOutput").ap()
    if _DBG_CHUNK >= 0:
        for nm, dt in (("d_ln1b", BF16), ("d_lnf", BF16), ("d_qf", BF16),
                       ("d_kf", BF16), ("d_ktm", BF16), ("d_vtm", BF16),
                       ("d_ytm", BF16), ("d_atb", BF16)):
            ap[nm] = nc.dram_tensor(nm, [128, 2048], dt,
                                    kind="ExternalOutput").ap()
    if _DBG_B >= 0:
        for nm in ("db_y", "db_xtf", "db_h"):
            ap[nm] = nc.dram_tensor(nm, [128, NTB * DM], F32,
                                    kind="ExternalOutput").ap()
        ap["db_st"] = nc.dram_tensor("db_st", [128, 64], F32,
                                     kind="ExternalOutput").ap()
        ap["db_timeb"] = nc.dram_tensor("db_timeb", [128, 8], F32,
                                        kind="ExternalOutput").ap()

        ap["db_gate"] = nc.dram_tensor("db_gate", [128, NTILE * NEXP], F32,
                                       kind="ExternalOutput").ap()
        ap["db_ue"] = nc.dram_tensor("db_ue", [128, 8], F32,
                                     kind="ExternalOutput").ap()
    ap["xprime"] = nc.dram_tensor("xprime", [NTOK, DM], F32,
                                  kind="Internal").ap()
    ap["ln3s"] = nc.dram_tensor("ln3s", [DM, NTOK], BF16,
                                kind="Internal").ap()

    with ExitStack() as es:
        E = es.enter_context
        sb = {}

        def S_(name, shape, dtype):
            sb[name] = E(nc.sbuf_tensor(name, shape, dtype))
            return sb[name]

        S_("gate", [128, NTILE * NEXP], F32)
        S_("maskut", [128, 128], F32)
        S_("ident", [128, 128], BF16)
        S_("onesc", [128, 2], BF16)
        S_("gw", [128, 8 * NEXP], BF16)
        S_("be1", [128, NEXP * 4], F32)
        S_("be3", [128, NEXP * 4], F32)
        S_("bs1", [128, 8], F32)
        S_("bs3", [128, 8], F32)
        S_("btw5x", [5, 2 * 4 * SI], BF16)
        S_("bt5", [5, 256], BF16)
        S_("onesf", [128, 8], F32)
        for i in range(16):
            S_("st%x" % i, [128, 16], F32)
        S_("v2c", [128, 8], F32)
        S_("dotc", [128, 8], F32)
        S_("lam", [128, 8], F32)
        S_("lami", [128, 8], F32)
        S_("ue", [128, 8], F32)
        S_("timeb", [128, 8], BF16)
        
        S_("Zf", [128, 8], F32)
        S_("Zb", [128, 8], BF16)
        S_("ceps5", [128, 1], F32)
        S_("ceps6", [128, 1], F32)
        S_("cone", [128, 1], F32)
        S_("cneg6", [128, 1], F32)

        AB_N = 50688
        AF_N = 21632
        ab_t = S_("arena_bf", [128, AB_N], BF16)
        af_t = S_("arena_f32", [128, AF_N], F32)

        PS = [E(nc.psum_tensor("ps%d" % i, [128, 512], F32)) for i in range(7)]
        PT = E(nc.psum_tensor("pst", [128, 1024], BF16))

        sem_names = ["sdt", "sdx", "sds", "sdr", "gd", "ts", "vs", "ss"]
        sem_h = {n: E(nc.semaphore(n)) for n in sem_names}
        block = E(nc.Block())

        def script(cur, engs):
            gp = EngP(engs.get("gpsimd"), cur == "gpsimd")
            sc = EngP(engs.get("scalar"), cur == "scalar")
            te = EngP(engs.get("tensor"), cur == "tensor")
            ve = EngP(engs.get("vector"), cur == "vector")
            sy = EngP(engs.get("sync"), cur == "sync")
            sm = {n: Sem(sem_h[n]) for n in sem_names}
            sdt, sdx, sds, sdr, gd, ts, vs, ss = (sm[n] for n in sem_names)
            guard = {}

            def db(eng, sem):
                if sem.n > 0:
                    W(eng, sem, sem.n)

            def G(eng, key):
                if key in guard:
                    s, n = guard[key]
                    W(eng, s, n)

            def setg(key, sem):
                guard[key] = (sem, sem.n)

            def FV(ins):
                I(vs, ins)
                W(ve, vs)

            A = Arena(ab_t, AB_N)
            Fa = Arena(af_t, AF_N)
            wq3 = A.take(8 * DM).rearrange("p (k m) -> p k m", k=8)
            wk3 = A.take(8 * DM).rearrange("p (k m) -> p k m", k=8)
            wv3 = A.take(8 * DM).rearrange("p (k m) -> p k m", k=8)
            wp3 = A.take(8 * DM).rearrange("p (k m) -> p k m", k=8)
            ln1b = A.take(2 * DM)
            lnf_a = A.take(8 * 128)
            lnf3a = lnf_a.rearrange("p (k n) -> p k n", k=8)
            lnf_b = A.take(8 * 128)
            lnf3b = lnf_b.rearrange("p (k n) -> p k n", k=8)
            g_lnf = A.take(8 * 128)
            g_lnf3 = g_lnf.rearrange("p (k n) -> p k n", k=8)
            qf = A.take(8 * 128)
            qf3 = qf.rearrange("p (k n) -> p k n", k=8)
            kf = A.take(8 * 128)
            kf3 = kf.rearrange("p (k n) -> p k n", k=8)
            ktm = A.take(DM)
            vtm = A.take(DM)
            atb = A.take(16 * 128)
            atb3 = atb.rearrange("p (h n) -> p h n", h=16)
            ytm = A.take(2 * DM)
            yfm = A.take(8 * 128)
            yfm3 = yfm.rearrange("p (k n) -> p k n", k=8)
            ln3b = A.take(2 * DM)
            Sb = A.take(512)

            xb = Fa.take(4 * DM)
            psb = Fa.take(DM)
            tmpa = Fa.take(DM)
            tmpb = Fa.take(DM)
            tmpc = Fa.take(DM)
            junk = Fa.take(DM)
            xpb = Fa.take(2 * DM)
            Sf = Fa.take(512)
            glog = Fa.take(NEXP)
            bpbc = Fa.take(DM)

            gate = sb["gate"][:, :]
            maskut = sb["maskut"][:, :]
            ident = sb["ident"][:, :]
            onesc = sb["onesc"][:, :]
            gw3 = sb["gw"][:, :].rearrange("p (k n) -> p k n", k=8)
            st = [sb["st%x" % i][:, :] for i in range(16)]
            Zf = sb["Zf"][:, :]
            Zb = sb["Zb"][:, :]
            ceps5 = sb["ceps5"][:, 0:1]
            ceps6 = sb["ceps6"][:, 0:1]
            cone = sb["cone"][:, 0:1]
            cneg6 = sb["cneg6"][:, 0:1]

            def sc_ln_stats(sum_col, ss_col, m_out, rstd_out, t0, t1):
                sc.mul(m_out, sum_col, 1.0 / DM)
                sc.mul(t0, ss_col, 1.0 / DM)
                sc.square(t1, m_out)
                sc.activation(t1, t1, AF.Identity, scale=-1.0, bias=ceps5)
                sc.activation(t0, t0, AF.Ln, bias=t1)
                return sc.activation(rstd_out, t0, AF.Exp, scale=-0.5)

            for w3, src in ((wq3, "wq_t"), (wk3, "wk_t"), (wv3, "wv_t"),
                            (wp3, "wp_t")):
                ins = gp.dma_start(
                    out=w3, in_=ap[src].rearrange("(k p) m -> p k m", p=128))
                I(gd, ins, 16)
            ins = gp.dma_start(
                out=gw3, in_=ap["gw_t"].rearrange("(k p) n -> p k n", p=128))
            I(gd, ins, 16)
            ins = gp.dma_start(out=sb["bt5"][:, :], in_=ap["bt5_init"])
            I(gd, ins, 16)
            for nm_, src in (("maskut", "mask_ut"), ("ident", "ident_bf"),
                             ("onesc", "onescol"),
                             ("be1", "be1_l"), ("be3", "be3_l"),
                             ("bs1", "bs1_l"), ("bs3", "bs3_l")):
                ins = sy.dma_start(out=sb[nm_][:, :], in_=ap[src])
                I(sds, ins, 16)
            ins = sy.dma_start(out=bpbc, in_=ap["bp_bc"])
            I(sds, ins, 16)
            pre_sd = sds.n
            pre_gd = gd.n
            ve.memset(sb["onesf"][:, :], 1.0)
            ve.memset(sb["ceps5"][:, :], LN_EPS)
            ve.memset(sb["ceps6"][:, :], ATT_EPS)
            ve.memset(sb["cone"][:, :], 1.0)
            ins = ve.memset(sb["cneg6"][:, :], -ATT_EPS)
            I(vs, ins)

            ins = sy.dma_start(out=xb[:, 0:DM], in_=ap["x"][0:128, :])
            I(sdx, ins, 16)
            xload = {0: sdx.n}

            W(te, sds, pre_sd)
            W(te, gd, pre_gd)
            W(ve, sds, pre_sd)
            W(sc, sds, pre_sd)
            W(sc, vs)

            lnb_done, lnf_ev, y_ev, ln3_done = {}, {}, {}, {}

            def x_view(c):
                return xb[:, (c % 4) * DM:(c % 4) * DM + DM]

            def emit_pre(c):
                if c + 1 < NTILE:
                    G(sy, "xb%d" % ((c + 1) % 4))
                    db(sy, sdx)
                    ins = sy.dma_start(
                        out=x_view(c + 1),
                        in_=ap["x"][(c + 1) * 128:(c + 2) * 128, :])
                    xload[c + 1] = I(sdx, ins, 16)
                xv = x_view(c)
                lv = ln1b[:, (c % 2) * DM:(c % 2) * DM + DM]
                lf3 = lnf3a if c % 2 == 0 else lnf3b
                W(ve, sdx, xload[c])
                ve.scalar_tensor_tensor(junk, xv, 1.0, xv, ALU.mult, ALU.mult,
                                        accum_out=st[0][:, 0:1])
                ins = ve.tensor_reduce(st[0][:, 1:2], xv, AX.X, ALU.add)
                I(vs, ins)
                W(sc, vs)
                ins = sc_ln_stats(st[0][:, 1:2], st[0][:, 0:1],
                                  st[0][:, 2:3], st[0][:, 3:4],
                                  st[0][:, 4:5], st[0][:, 5:6])
                I(ss, ins)
                W(ve, ss)
                G(ve, "ln1b%d" % (c % 2))
                ins = ve.tensor_scalar(lv, xv, st[0][:, 2:3], st[0][:, 3:4],
                                       ALU.subtract, ALU.mult)
                lnb_done[c] = I(vs, ins)

            def emit_lnf_tp(c):
                lv = ln1b[:, (c % 2) * DM:(c % 2) * DM + DM]
                lf = lnf_a if c % 2 == 0 else lnf_b
                G(te, "PT")
                W(te, vs, lnb_done[c])
                for j in range(8):
                    ins = te.transpose(PT[:, j * 128:(j + 1) * 128],
                                       lv[:, j * 128:(j + 1) * 128], ident)
                tpn = I(ts, ins)
                setg("ln1b%d" % (c % 2), ts)
                W(sc, ts, tpn)
                G(sc, "lnf%d" % (c % 2))
                sc.copy(lf[:, 0:512], PT[:, 0:512])
                ins = sc.copy(lf[:, 512:1024], PT[:, 512:1024])
                lnf_ev[c] = I(ss, ins)
                setg("PT", ss)

            def emit_qkv(c):
                lf3 = lnf3a if c % 2 == 0 else lnf3b
                W(te, ss, lnf_ev[c])
                for half in range(2):
                    b = (2, 4)[half]
                    G(te, "P%d" % b)
                    for mi in range(4):
                        m = half * 4 + mi
                        for k in range(8):
                            ins = te.matmul(
                                PS[b][:, mi * 128:(mi + 1) * 128],
                                wk3[:, k, m * 128:(m + 1) * 128],
                                lf3[:, k, :], start=(k == 0), stop=(k == 7))
                    kn = I(ts, ins)
                    tmp = tmpa if half == 0 else tmpb
                    W(sc, ts, kn)
                    G(sc, "phitmp%d" % half)
                    sc.activation(tmp[:, 0:512], PS[b][:, 0:512], AF.Relu,
                                  scale=-1.0)
                    ins = sc.activation(tmp[:, 0:512], tmp[:, 0:512], AF.Exp,
                                        scale=-1.0)
                    I(ss, ins)
                    W(ve, ss)
                    G(ve, "kf")
                    ins = ve.scalar_tensor_tensor(
                        kf[:, half * 512:half * 512 + 512], PS[b][:, 0:512],
                        0.0, tmp[:, 0:512], ALU.max, ALU.add)
                    I(vs, ins)
                    setg("P%d" % b, vs)
                    setg("phitmp%d" % half, vs)
                kf_done = vs.n
                for half in range(2):
                    b = (0, 1)[half]
                    G(te, "P%d" % b)
                    for mi in range(4):
                        m = half * 4 + mi
                        for k in range(8):
                            ins = te.matmul(
                                PS[b][:, mi * 128:(mi + 1) * 128],
                                wq3[:, k, m * 128:(m + 1) * 128],
                                lf3[:, k, :], start=(k == 0), stop=(k == 7))
                    qn = I(ts, ins)
                    tmp = tmpa if half == 0 else tmpb
                    W(sc, ts, qn)
                    G(sc, "phitmp%d" % half)
                    sc.activation(tmp[:, 0:512], PS[b][:, 0:512], AF.Relu,
                                  scale=-1.0)
                    ins = sc.activation(tmp[:, 0:512], tmp[:, 0:512], AF.Exp,
                                        scale=-1.0)
                    I(ss, ins)
                    W(ve, ss)
                    G(ve, "qf")
                    ins = ve.scalar_tensor_tensor(
                        qf[:, half * 512:half * 512 + 512], PS[b][:, 0:512],
                        0.0, tmp[:, 0:512], ALU.max, ALU.add)
                    I(vs, ins)
                    setg("P%d" % b, vs)
                    setg("phitmp%d" % half, vs)
                G(te, "PT")
                W(te, vs, kf_done)
                for j in range(8):
                    ins = te.transpose(PT[:, j * 128:(j + 1) * 128],
                                       kf3[:, j, :], ident)
                ktpn = I(ts, ins)
                W(sc, ts, ktpn)
                G(sc, "ktm")
                sc.copy(ktm[:, 0:512], PT[:, 0:512])
                ins = sc.copy(ktm[:, 512:1024], PT[:, 512:1024])
                ktm_ev = I(ss, ins)
                setg("PT", ss)
                for half in range(2):
                    b = (5, 6)[half]
                    G(te, "P%d" % b)
                    for k in range(8):
                        ins = te.matmul(PS[b][:, 0:512], lf3[:, k, :],
                                        wv3[:, k, half * 512:half * 512 + 512],
                                        start=(k == 0), stop=(k == 7))
                    vn = I(ts, ins)
                    W(sc, ts, vn)
                    G(sc, "vtm")
                    ins = sc.activation(vtm[:, half * 512:half * 512 + 512],
                                        PS[b][:, 0:512], AF.Copy)
                    I(ss, ins)
                    setg("P%d" % b, ss)
                setg("lnf%d" % (c % 2), ts)
                return ktm_ev

            def emit_heads(c, ktm_dn):
                n_in_seq = c % 16
                if n_in_seq == 0:
                    G(ve, "Sstate")
                    ve.memset(Sf, 0.0)
                    ve.memset(Zf, 0.0)
                    ve.memset(Sb, 0.0)
                    ins = ve.memset(Zb, 0.0)
                    I(vs, ins)
                W(te, ss, ktm_dn)
                yv = ytm[:, (c % 2) * DM:(c % 2) * DM + DM]
                atb_v = {}

                def emit_at(h):
                    hh, po = h // 2, (h % 2) * 64
                    pA = 4 + (h % 2)
                    G(te, "P%d" % pA)
                    ins = te.matmul(PS[pA][:, 0:128], kf3[po:po + 64, hh, :],
                                    qf3[po:po + 64, hh, :], start=True,
                                    stop=True)
                    atn = I(ts, ins)
                    W(ve, ts, atn)
                    ins = ve.tensor_tensor(atb3[:, h, :], PS[pA][:, 0:128],
                                           maskut, ALU.mult)
                    atb_v[h] = I(vs, ins)
                    setg("P%d" % pA, vs)

                emit_at(0)
                for h in range(16):
                    hh, po = h // 2, (h % 2) * 64
                    pKV = 0 + (h % 2)
                    pN = 6 if h % 2 == 0 else 2
                    if h + 1 < 16:
                        emit_at(h + 1)
                    G(te, "P%d" % pKV)
                    te.matmul(PS[pKV][po:po + 64, 0:64],
                              ktm[:, h * 64:h * 64 + 64],
                              vtm[:, h * 64:h * 64 + 64], start=True,
                              stop=True)
                    te.matmul(PS[pKV][po:po + 64, 64:65],
                              ktm[:, h * 64:h * 64 + 64], onesc[:, 0:1],
                              start=True, stop=True)
                    G(te, "P%d" % pN)
                    W(te, vs, atb_v[h])
                    te.matmul(PS[pN][:, 0:64], atb3[:, h, :],
                              vtm[:, h * 64:h * 64 + 64], start=True,
                              stop=False)
                    te.matmul(PS[pN][:, 0:64], qf3[po:po + 64, hh, :],
                              Sb[po:po + 64, hh * 64:hh * 64 + 64],
                              start=False, stop=True)
                    te.matmul(PS[pN][:, 64:65], atb3[:, h, :], onesc[:, 0:1],
                              start=True, stop=False)
                    ins = te.matmul(PS[pN][:, 64:65], qf3[po:po + 64, hh, :],
                                    Zb[po:po + 64, hh:hh + 1],
                                    start=False, stop=True)
                    hdn = I(ts, ins)
                    s4 = st[1][:, (h % 2):(h % 2) + 1]
                    s6 = st[1][:, 2 + (h % 2):3 + (h % 2)]
                    W(sc, ts, hdn)
                    G(sc, "den%d" % (h % 2))
                    ins = sc.activation(s4, PS[pN][:, 64:65], AF.Identity,
                                        bias=ceps6)
                    I(ss, ins)
                    W(ve, ss)
                    ins = ve.reciprocal(s6, s4)
                    I(vs, ins)
                    W(sc, vs)
                    if h == 0:
                        G(sc, "ytm%d" % (c % 2))
                    ins = sc.activation(yv[:, h * 64:h * 64 + 64],
                                        PS[pN][:, 0:64], AF.Identity,
                                        scale=s6)
                    I(ss, ins)
                    setg("P%d" % pN, ss)
                    setg("den%d" % (h % 2), ss)
                    ve.tensor_tensor(Sf[po:po + 64, hh * 64:hh * 64 + 64],
                                     Sf[po:po + 64, hh * 64:hh * 64 + 64],
                                     PS[pKV][po:po + 64, 0:64], ALU.add)
                    ve.tensor_tensor(Zf[po:po + 64, hh:hh + 1],
                                     Zf[po:po + 64, hh:hh + 1],
                                     PS[pKV][po:po + 64, 64:65], ALU.add)
                    ve.tensor_copy(Sb[po:po + 64, hh * 64:hh * 64 + 64],
                                   Sf[po:po + 64, hh * 64:hh * 64 + 64])
                    ins = ve.tensor_copy(Zb[po:po + 64, hh:hh + 1],
                                         Zf[po:po + 64, hh:hh + 1])
                    I(vs, ins)
                    setg("P%d" % pKV, vs)
                setg("Sstate", ts)
                setg("ktm", ts)
                setg("vtm", ts)
                setg("qf", ts)
                setg("kf", ts)
                y_ev[c] = ss.n
                if c + 1 < NTILE:
                    emit_lnf_tp(c + 1)

            def emit_proj_mm(c):
                yv_ = ytm[:, (c % 2) * DM:(c % 2) * DM + DM]
                G(te, "PT")
                W(te, ss, y_ev[c])
                for j in range(8):
                    ins = te.transpose(PT[:, j * 128:(j + 1) * 128],
                                       yv_[:, j * 128:(j + 1) * 128], ident)
                ytpn = I(ts, ins)
                setg("ytm%d" % (c % 2), ts)
                W(sc, ts, ytpn)
                G(sc, "yfm")
                sc.copy(yfm[:, 0:512], PT[:, 0:512])
                ins = sc.copy(yfm[:, 512:1024], PT[:, 512:1024])
                yfm_ev = I(ss, ins)
                setg("PT", ss)
                W(te, ss, yfm_ev)
                pj = []
                for half in range(2):
                    G(te, "P3")
                    for k in range(8):
                        ins = te.matmul(PS[3][:, 0:512], yfm3[:, k, :],
                                        wp3[:, k, half * 512:half * 512 + 512],
                                        start=(k == 0), stop=(k == 7))
                    pj.append(I(ts, ins))
                    W(ve, ts, pj[-1])
                    G(ve, "psb")
                    ins = ve.scalar_tensor_tensor(
                        psb[:, half * 512:half * 512 + 512], PS[3][:, 0:512],
                        1.0, bpbc[:, half * 512:half * 512 + 512], ALU.mult,
                        ALU.add, accum_out=st[2][:, half:half + 1])
                    I(vs, ins)
                    setg("P3", vs)
                setg("yfm", ts)

            def emit_proj_tail(c):
                ve.scalar_tensor_tensor(junk[:, 0:512], psb[:, 0:512], 1.0,
                                        psb[:, 0:512], ALU.mult, ALU.mult,
                                        accum_out=st[2][:, 2:3])
                ins = ve.scalar_tensor_tensor(junk[:, 512:1024],
                                              psb[:, 512:1024],
                                              psb[:, 512:1024], None,
                                              ALU.bypass, ALU.mult) \
                    if False else ve.scalar_tensor_tensor(
                        junk[:, 512:1024], psb[:, 512:1024], 1.0,
                        psb[:, 512:1024], ALU.mult, ALU.mult,
                        accum_out=st[2][:, 3:4])
                I(vs, ins)
                W(sc, vs)
                sc.activation(st[2][:, 0:1], st[2][:, 0:1], AF.Identity,
                              bias=st[2][:, 1:2])
                sc.activation(st[2][:, 2:3], st[2][:, 2:3], AF.Identity,
                              bias=st[2][:, 3:4])
                ins = sc_ln_stats(st[2][:, 0:1], st[2][:, 2:3],
                                  st[2][:, 4:5], st[2][:, 5:6],
                                  st[2][:, 6:7], st[2][:, 7:8])
                I(ss, ins)
                W(ve, ss)
                xv = x_view(c)
                xpv = xpb[:, (c % 2) * DM:(c % 2) * DM + DM]
                G(ve, "xp%d" % (c % 2))
                ins = ve.tensor_scalar(tmpc, psb, st[2][:, 4:5],
                                       st[2][:, 5:6], ALU.subtract, ALU.mult)
                FV(ins)
                ins = ve.tensor_tensor(xpv, tmpc, xv, ALU.add)
                FV(ins)
                setg("xb%d" % (c % 4), vs)
                setg("psb", vs)
                ve.tensor_reduce(st[3][:, 0:1], xpv, AX.X, ALU.add)
                ins = ve.scalar_tensor_tensor(junk, xpv, 1.0, xpv, ALU.mult,
                                              ALU.mult,
                                              accum_out=st[3][:, 1:2])
                I(vs, ins)
                W(sc, vs)
                ins = sc_ln_stats(st[3][:, 0:1], st[3][:, 1:2],
                                  st[3][:, 2:3], st[3][:, 3:4],
                                  st[3][:, 4:5], st[3][:, 5:6])
                I(ss, ins)
                W(ve, ss)
                ln3v = ln3b[:, (c % 2) * DM:(c % 2) * DM + DM]
                G(ve, "l3%d" % (c % 2))
                ins = ve.tensor_scalar(ln3v, xpv, st[3][:, 2:3],
                                       st[3][:, 3:4], ALU.subtract, ALU.mult)
                I(vs, ins)
                W(sy, vs)
                db(sy, sds)
                ins = sy.dma_start(out=ap["xprime"][c * 128:(c + 1) * 128, :],
                                   in_=xpv)
                I(sds, ins, 16)
                if not phase_b:
                    ins = sy.dma_start(out=ap["out"][c * 128:(c + 1) * 128, :],
                                       in_=xpv)
                    I(sds, ins, 16)
                setg("xp%d" % (c % 2), sds)
                ln3_done[c] = vs.n

            def emit_gate(c):
                ln3v = ln3b[:, (c % 2) * DM:(c % 2) * DM + DM]
                G(te, "PT")
                W(te, vs, ln3_done[c])
                for j in range(8):
                    ins = te.transpose(PT[:, j * 128:(j + 1) * 128],
                                       ln3v[:, j * 128:(j + 1) * 128], ident)
                gtpn = I(ts, ins)
                setg("l3%d" % (c % 2), ts)
                W(sc, ts, gtpn)
                G(sc, "g_lnf")
                G(sc, "g_lnfs")
                sc.copy(g_lnf[:, 0:512], PT[:, 0:512])
                ins = sc.copy(g_lnf[:, 512:1024], PT[:, 512:1024])
                g_ev = I(ss, ins)
                setg("PT", ss)
                W(sy, ss, g_ev)
                db(sy, sds)
                for k in range(8):
                    ins = sy.dma_start(
                        out=ap["ln3s"][k * 128:(k + 1) * 128,
                                       c * 128:(c + 1) * 128],
                        in_=g_lnf3[:, k, :])
                    I(sds, ins, 16)
                setg("g_lnfs", sds)
                W(te, ss, g_ev)
                G(te, "P3")
                for k in range(8):
                    ins = te.matmul(PS[3][:, 0:NEXP], g_lnf3[:, k, :],
                                    gw3[:, k, :], start=(k == 0), stop=(k == 7))
                gmm = I(ts, ins)
                setg("g_lnf", ts)
                W(ve, ts, gmm)
                ve.tensor_copy(glog, PS[3][:, 0:NEXP])
                ins = ve.reduce_max(st[4][:, 0:1], PS[3][:, 0:NEXP], AX.X)
                I(vs, ins)
                setg("P3", vs)
                W(sc, vs)
                sc.mul(st[4][:, 1:2], st[4][:, 0:1], -1.0)
                ins = sc.activation(st[5][:, 0:NEXP], glog, AF.Exp,
                                    bias=st[4][:, 1:2],
                                    accum_out=st[4][:, 2:3])
                I(ss, ins)
                W(ve, ss)
                ve.reciprocal(st[4][:, 3:4], st[4][:, 2:3])
                ins = ve.tensor_scalar(st[6][:, 0:NEXP], glog, st[4][:, 0:1],
                                       None, ALU.is_equal)
                FV(ins)
                ve.scalar_tensor_tensor(st[7][:, 0:NEXP], st[6][:, 0:NEXP],
                                        -1e30, glog, ALU.mult, ALU.add)
                ve.tensor_scalar(st[5][:, 0:NEXP], st[5][:, 0:NEXP],
                                 st[4][:, 3:4], None, ALU.mult)
                ins = ve.reduce_max(st[4][:, 4:5], st[7][:, 0:NEXP], AX.X)
                FV(ins)
                ins = ve.tensor_scalar(st[7][:, 0:NEXP], st[7][:, 0:NEXP],
                                       st[4][:, 4:5], None, ALU.is_equal)
                FV(ins)
                ins = ve.tensor_tensor(st[6][:, 0:NEXP], st[6][:, 0:NEXP],
                                       st[7][:, 0:NEXP], ALU.add)
                FV(ins)
                ins = ve.tensor_tensor(gate[:, c * NEXP:(c + 1) * NEXP],
                                       st[5][:, 0:NEXP], st[6][:, 0:NEXP],
                                       ALU.mult)
                I(vs, ins)

            emit_pre(0)
            emit_lnf_tp(0)
            for c in range(NTILE):
                if c + 1 < NTILE:
                    emit_pre(c + 1)
                kdn = emit_qkv(c)
                if c >= 2:
                    emit_gate(c - 2)
                if c >= 1:
                    emit_proj_mm(c - 1)
                emit_heads(c, kdn)
                if c >= 1:
                    emit_proj_tail(c - 1)
            emit_gate(NTILE - 2)
            emit_proj_mm(NTILE - 1)
            emit_proj_tail(NTILE - 1)
            emit_gate(NTILE - 1)

            a_end = {k: sm[k].n for k in sem_names}

            if not phase_b:
                W(sy, sds)
                return

            for eng in (gp, sc, te, ve, sy):
                for k in sem_names:
                    W(eng, sm[k], a_end[k])

            B = Arena(ab_t, AB_N)
            Fb = Arena(af_t, AF_N)
            xtf = B.take(8 * TCB)
            xtf3 = xtf.rearrange("p (k n) -> p k n", k=8)
            slot = B.take(2 * ESLOT)
            hb = B.take(8 * TCB)
            hb3 = hb.rearrange("p (m n) -> p m n", m=8)
            hq = B.take(8 * 512)
            hq3 = hq.rearrange("p (m n) -> p m n", m=8)
            junkb = B.take(SI + 1)

            yb = Fb.take(NTB * DM)
            yb3 = yb.rearrange("p (t f) -> p t f", t=NTB)
            asb = Fb.take(8 * 512)
            asb3 = asb.rearrange("p (m n) -> p m n", m=8)
            xpl = Fb.take(2 * DM)
            outb = Fb.take(2 * DM)
            junk2 = Fb.take(DM)
            junk3 = Fb.take(DM)
            hcp = Fb.take(2 * DM)

            btw5x = sb["btw5x"][:, :].rearrange("p (s j n) -> p s j n", s=2, j=4)
            bt5 = sb["bt5"][:, :]
            be1 = sb["be1"][:, :]
            be3 = sb["be3"][:, :]
            bs1 = sb["bs1"][:, :]
            bs3 = sb["bs3"][:, :]
            v2c = sb["v2c"][:, :]
            dotc = sb["dotc"][:, :]
            lam = sb["lam"][:, :]
            lami = sb["lami"][:, :]
            uev = sb["ue"][:, :]
            timeb = sb["timeb"][:, :]
            onesf = sb["onesf"][:, :]
            gate_r = sb["gate"][:, :].rearrange("p (t e) -> p t e", t=NTILE)


            xt_pref = {}

            def emit_xt_load(bc):
                G(gp, "xtf")
                db(gp, sdt)
                for k in range(8):
                    ins = gp.dma_start(
                        out=xtf3[:, k, :],
                        in_=ap["ln3s"][k * 128:(k + 1) * 128,
                                       bc * TCB:(bc + 1) * TCB])
                    I(sdt, ins, 16)
                xt_pref[bc] = sdt.n

            emit_xt_load(0)
            pending_v = []
            for bc in range(NBCHUNK):
                G(ve, "hb3")
                G(ve, "hb7")
                ve.memset(hb3[:, 3, :], 0.0)
                ins = ve.memset(hb3[:, 7, :], 0.0)
                I(vs, ins)
                xt_done = xt_pref[bc]

                ve.memset(yb, 0.0)
                ve.memset(lami, 1.0)
                ve.memset(yb3[:, :, 0:1], 1.0)
                ins = ve.memset(lam, 1.0)
                I(vs, ins)

                for t in range(2):
                    G(sy, "xpl%d" % (t % 2))
                    db(sy, sdx)
                    ins = sy.dma_start(
                        out=xpl[:, (t % 2) * DM:(t % 2) * DM + DM],
                        in_=ap["xprime"][(bc * NTB + t) * 128:
                                         (bc * NTB + t + 1) * 128, :])
                    I(sdx, ins, 16)
                xpl_n = {0: sdx.n - 16, 1: sdx.n}

                for e in range(NEXP + 1):
                    shared = (e == NEXP)
                    nm = 8 if shared else 4
                    k2 = 8 if shared else 4
                    if shared:
                        w1v = slot[:, 0:8 * SI].rearrange(
                            "p (k m) -> p k m", k=8)
                        w3v = slot[:, 8 * SI:16 * SI].rearrange(
                            "p (k m) -> p k m", k=8)
                        w2v = slot[:, 16 * SI:24 * SI].rearrange(
                            "p (k m) -> p k m", k=8)
                    else:
                        base = (e % 2) * ESLOT
                        w1v = slot[:, base:base + 8 * EI].rearrange(
                            "p (k m) -> p k m", k=8)
                        w3v = slot[:, base + 8 * EI:base + 16 * EI].rearrange(
                            "p (k m) -> p k m", k=8)
                        w2v = slot[:, base + 16 * EI:base + 16 * EI + 4 * SI
                                   ].rearrange("p (k m) -> p k m", k=4)

                    if shared:
                        for kk in ("slotA0", "slotA1", "slotB0", "slotB1",
                                   "btw5x0", "btw5x1"):
                            G(gp, kk)
                        db(gp, gd)
                        for wv_, src in ((w1v, "ws1_t"), (w3v, "ws3_t")):
                            ins = gp.dma_start(out=wv_, in_=ap[src].rearrange(
                                "(k p) m -> p k m", p=128))
                            I(gd, ins, 16)
                        w13_rdy = gd.n
                        db(gp, gd)
                        ins = gp.dma_start(out=w2v, in_=ap["ws2p_t"].rearrange(
                            "(k p) m -> p k m", p=128))
                        I(gd, ins, 16)
                        db(gp, gd)
                        ins = gp.dma_start(
                            out=btw5x[:, 0, :, :],
                            in_=ap["btw5s"].rearrange("j p n -> p j n"))
                        I(gd, ins, 16)
                    else:
                        G(gp, "slotA%d" % (e % 2))
                        db(gp, gd)
                        for wv_, src in ((w1v, "we1_t"), (w3v, "we3_t")):
                            ins = gp.dma_start(
                                out=wv_,
                                in_=ap[src][e].rearrange("(k p) m -> p k m",
                                                         p=128))
                            I(gd, ins, 16)
                        w13_rdy = gd.n
                        G(gp, "slotB%d" % (e % 2))
                        db(gp, gd)
                        ins = gp.dma_start(
                            out=w2v,
                            in_=ap["we2p_t"][e].rearrange("(k p) m -> p k m",
                                                          p=128))
                        I(gd, ins, 16)
                        G(gp, "btw5x%d" % (e % 2))
                        db(gp, gd)
                        ins = gp.dma_start(
                            out=btw5x[:, e % 2, :, :],
                            in_=ap["btw5e"][e].rearrange("j p n -> p j n"))
                        I(gd, ins, 16)
                    w_rdy = gd.n

                    def uevblk(shared=shared, e=e, bc=bc):
                        if shared:
                            ins = ve.tensor_copy(uev, lami)
                        else:
                            ins = ve.tensor_tensor(
                                uev,
                                gate_r[:, bc * NTB:(bc + 1) * NTB, e:e + 1
                                       ].rearrange("p t o -> p (t o)"),
                                lami, ALU.mult)
                        I(vs, ins)
                    pending_v.append(uevblk)

                    W(te, gd, w13_rdy)
                    W(te, sdt, xt_done)
                    sq_f = [0, 0]
                    hvn_m = {}
                    for f in range(2):
                        for m in range(nm):
                            rows = 128 if m < nm - 1 else 127
                            mt = f * nm + m
                            bB = (m % 2)
                            aB = (2, 3)[mt % 2]
                            G(te, "P%d" % bB)
                            for k in range(8):
                                ins = te.matmul(
                                    PS[bB][0:rows, 0:512],
                                    w3v[:, k, m * 128:m * 128 + rows],
                                    xtf3[:, k, f * 512:f * 512 + 512],
                                    start=(k == 0), stop=(k == 7))
                            bmm = I(ts, ins)
                            G(te, "P%d" % aB)
                            for k in range(8):
                                ins = te.matmul(
                                    PS[aB][0:rows, 0:512],
                                    w1v[:, k, m * 128:m * 128 + rows],
                                    xtf3[:, k, f * 512:f * 512 + 512],
                                    start=(k == 0), stop=(k == 7))
                            amm = I(ts, ins)
                            if f == 1 and m == nm - 1:
                                setg("slotA%d" % (0 if shared else e % 2), ts)
                                if shared:
                                    setg("slotA1", ts)
                                    setg("xtf", ts)
                            bcol = (bs1[:, m:m + 1] if shared
                                    else be1[:, e * 4 + m:e * 4 + m + 1])
                            W(sc, ts, amm)
                            G(sc, "asb%d" % m)
                            ins = sc.activation(asb3[0:rows, m, :],
                                                PS[aB][0:rows, 0:512],
                                                AF.Silu, bias=bcol[0:rows, :])
                            I(ss, ins)
                            setg("P%d" % aB, ss)
                            b3col = (bs3[:, m:m + 1] if shared
                                     else be3[:, e * 4 + m:e * 4 + m + 1])
                            W(ve, ss)
                            W(ve, ts, bmm)
                            G(ve, "hb%d" % m)
                            ins = ve.scalar_tensor_tensor(
                                hb3[0:rows, m, f * 512:f * 512 + 512],
                                PS[bB][0:rows, 0:512], b3col[0:rows, :],
                                asb3[0:rows, m, :], ALU.add, ALU.mult)
                            hvn = I(vs, ins)
                            hvn_m[(f, m)] = hvn
                            setg("P%d" % bB, vs)
                            setg("asb%d" % m, vs)
                            if f == 0:
                                W(sc, vs, hvn)
                                G(sc, "hq%d" % m)
                                ins = sc.activation(
                                    hq3[:, m, :],
                                    hb3[:, m, 0:512], AF.Square)
                                I(ss, ins)
                            if pending_v:
                                pending_v.pop(0)()
                        if f == 0:
                            hq0_ss = ss.n
                    b2r5 = btw5x[:, 0 if shared else e % 2, :, :]

                    def emit_gg_h(gg):
                        p0 = 5 if gg % 2 == 0 else 3
                        p1 = 6 if gg % 2 == 0 else 2
                        G(te, "P%d" % p0)
                        G(te, "P%d" % p1)
                        for half, (n0, nw, pb) in enumerate(
                                [(0, 512, p0), (512, 511, p1)]):
                            for k in range(k2):
                                te.matmul(
                                    PS[pb][:, 0:nw],
                                    hb3[:, k, gg * 128:(gg + 1) * 128],
                                    w2v[:, k, n0:n0 + nw],
                                    start=(k == 0), stop=False)

                    def emit_gg(gg):
                        p0 = 5 if gg % 2 == 0 else 3
                        p1 = 6 if gg % 2 == 0 else 2
                        W(te, ss, tr0 if gg < 4 else tr1)
                        hcol = 0 if gg < 4 else 128
                        for half, (n0, nw, pb) in enumerate(
                                [(0, 512, p0), (512, 511, p1)]):
                            ins2 = te.matmul(PS[pb][:, 0:nw],
                                             bt5[0:5, hcol:hcol + 128],
                                             b2r5[0:5, gg % 4, n0:n0 + nw],
                                             start=False, stop=True)
                        w2n = I(ts, ins2)
                        hv = hcp[:, (gg % 2) * DM:(gg % 2) * DM + SI]
                        W(sc, ts, w2n)
                        G(sc, "hcp%d" % (gg % 2))
                        sc.copy(hv[:, 0:512], PS[p0][:, 0:512])
                        ins2 = sc.copy(hv[:, 512:1023], PS[p1][:, 0:511])
                        I(ss, ins2)
                        setg("P%d" % p0, ss)
                        setg("P%d" % p1, ss)
                        ins2 = sc.activation(junkb[:, 0:SI], hv[:, 0:SI],
                                             AF.Square,
                                             accum_out=v2c[:, gg:gg + 1])
                        ssn2 = I(ss, ins2)

                        def vblock(gg=gg, hv=hv, ssn2=ssn2):
                            W(ve, ss, ssn2)
                            ve.scalar_tensor_tensor(
                                junk2[:, 0:512], hv[:, 0:512], 1.0,
                                yb3[:, gg, 1:513], ALU.mult, ALU.mult,
                                accum_out=st[6][:, 0:1])
                            ve.scalar_tensor_tensor(
                                junk3[:, 0:511], hv[:, 512:1023], 1.0,
                                yb3[:, gg, 513:1024], ALU.mult, ALU.mult,
                                accum_out=st[7][:, 0:1])
                            ve.scalar_tensor_tensor(
                                yb3[:, gg, 1:513], hv[:, 0:512],
                                uev[:, gg:gg + 1], yb3[:, gg, 1:513],
                                ALU.mult, ALU.add)
                            ve.scalar_tensor_tensor(
                                yb3[:, gg, 513:1024], hv[:, 512:1023],
                                uev[:, gg:gg + 1], yb3[:, gg, 513:1024],
                                ALU.mult, ALU.add)
                            ins3 = ve.tensor_tensor(dotc[:, gg:gg + 1],
                                                    st[6][:, 0:1],
                                                    st[7][:, 0:1],
                                                    ALU.add)
                            I(vs, ins3)
                            setg("hcp%d" % (gg % 2), vs)

                        if gg >= 6:
                            pending_v.append(vblock)
                        else:
                            vblock()

                    W(te, ss, hq0_ss)
                    G(te, "P4s")
                    for gg in range(4):
                        for m in range(nm):
                            cc = 1 if m == nm - 1 else 0
                            ins = te.matmul(
                                PS[4][:, gg:gg + 1],
                                hq3[:, m, gg * 128:(gg + 1) * 128],
                                onesc[:, cc:cc + 1],
                                start=(m == 0), stop=(m == nm - 1))
                    ssq0 = I(ts, ins)
                    W(te, gd, w_rdy)
                    emit_gg_h(0)
                    W(sc, ts, ssq0)
                    G(sc, "timeb")
                    ins = sc.activation(timeb[:, 0:4], PS[4][:, 0:4],
                                        AF.Sqrt, bias=cone)
                    sq_f[0] = I(ss, ins)
                    W(te, ss, sq_f[0])
                    G(te, "PT")
                    ins = te.transpose(PT[0:4, 0:128], timeb[:, 0:4], ident)
                    ttn0 = I(ts, ins)
                    W(sc, ts, ttn0)
                    G(sc, "bt5h0")
                    ins = sc.copy(bt5[0:4, 0:128], PT[0:4, 0:128])
                    tr0 = I(ss, ins)
                    setg("PT", ss)
                    hq1_ss = 0
                    for m in range(nm):
                        W(sc, vs, hvn_m[(1, m)])
                        G(sc, "hq%d" % m)
                        ins = sc.activation(hq3[:, m, :],
                                            hb3[:, m, 512:1024], AF.Square)
                        hq1_ss = I(ss, ins)

                    emit_gg(0)
                    emit_gg_h(1)
                    W(te, ss, max(sq_f[0], hq1_ss))
                    for gg in range(4):
                        for m in range(nm):
                            cc = 1 if m == nm - 1 else 0
                            ins = te.matmul(
                                PS[4][:, 4 + gg:5 + gg],
                                hq3[:, m, gg * 128:(gg + 1) * 128],
                                onesc[:, cc:cc + 1],
                                start=(m == 0), stop=(m == nm - 1))
                    ssq1 = I(ts, ins)
                    W(sc, ts, ssq1)
                    ins = sc.activation(timeb[:, 4:8], PS[4][:, 4:8],
                                        AF.Sqrt, bias=cone)
                    sq_f[1] = I(ss, ins)
                    setg("P4s", ss)
                    emit_gg(1)
                    emit_gg_h(2)
                    W(te, ss, sq_f[1])
                    G(te, "PT")
                    ins = te.transpose(PT[0:4, 0:128], timeb[:, 4:8], ident)
                    ttn1 = I(ts, ins)
                    setg("timeb", ts)
                    W(sc, ts, ttn1)
                    G(sc, "bt5h1")
                    ins = sc.copy(bt5[0:4, 128:256], PT[0:4, 0:128])
                    tr1 = I(ss, ins)
                    setg("PT", ss)
                    emit_gg(2)
                    emit_gg_h(3)
                    emit_gg(3)
                    for gg in range(4, NTB):
                        emit_gg_h(gg)
                        emit_gg(gg)
                    for m in range(nm):
                        setg("hb%d" % m, ts)
                    setg("hb3", ts)
                    setg("hb7", ts)
                    setg("slotB%d" % (0 if shared else e % 2), ts)
                    if shared:
                        setg("slotB1", ts)
                    setg("bt5h0", ts)
                    setg("bt5h1", ts)
                    setg("btw5x%d" % (e % 2), ts)

                    W(sc, ss)
                    ins = sc.activation(st[8][:, 0:8], v2c[:, 0:8], AF.Sqrt,
                                        bias=cone)
                    t2n = I(ss, ins)

                    def s2blk(t2n=t2n, shared=shared, e=e, bc=bc):
                        W(ve, ss, t2n)
                        y0v = yb3[:, :, 0:1].rearrange("p t o -> p (t o)")
                        wvv = onesf if shared else gate_r[
                            :, bc * NTB:(bc + 1) * NTB, e:e + 1].rearrange(
                            "p t o -> p (t o)")
                        ve.tensor_tensor(st[9][:, 0:8], y0v, st[8][:, 0:8],
                                         ALU.mult)
                        ve.tensor_tensor(st[10][:, 0:8], wvv, wvv, ALU.mult)
                        ve.tensor_tensor(st[9][:, 0:8], dotc[:, 0:8],
                                         st[9][:, 0:8], ALU.subtract)
                        ve.tensor_scalar(st[10][:, 0:8], st[10][:, 0:8], 1.0,
                                         None, ALU.add)
                        ve.tensor_tensor(st[9][:, 0:8], lam, st[9][:, 0:8],
                                         ALU.mult)
                        ve.tensor_tensor(st[11][:, 0:8], uev, st[8][:, 0:8],
                                         ALU.mult)
                        ve.tensor_tensor(st[9][:, 0:8], wvv, st[9][:, 0:8],
                                         ALU.mult)
                        ins = ve.tensor_tensor(y0v, y0v, st[11][:, 0:8],
                                               ALU.add)
                        FV(ins)
                        ins = ve.scalar_tensor_tensor(st[12][:, 0:8],
                                                      st[9][:, 0:8], -2.0,
                                                      st[10][:, 0:8],
                                                      ALU.mult, ALU.add)
                        I(vs, ins)
                        W(sc, vs)
                        sc.activation(st[13][:, 0:8], st[12][:, 0:8], AF.Relu,
                                      bias=cneg6)
                        ins = sc.activation(st[13][:, 0:8], st[13][:, 0:8],
                                            AF.Sqrt, bias=ceps6)
                        sqn = I(ss, ins)
                        W(ve, ss, sqn)
                        ve.reciprocal(st[14][:, 0:8], st[13][:, 0:8])
                        ve.tensor_tensor(lami, lami, st[13][:, 0:8], ALU.mult)
                        ins = ve.tensor_tensor(lam, lam, st[14][:, 0:8],
                                               ALU.mult)
                        I(vs, ins)
                    pending_v.append(s2blk)

                if False:
                    W(sy, vs)
                    W(sy, ss)
                    db(sy, sds)
                    I(sds, sy.dma_start(out=ap["db_y"], in_=yb), 16)
                    db(gp, gd)
                    I(gd, gp.dma_start(out=ap["db_xtf"], in_=xtf), 16)
                    I(gd, gp.dma_start(out=ap["db_h"], in_=hb), 16)
                    W(sy, gd)
                    for ii, bufv in enumerate((v2c, dotc, lam, lami, uev,
                                               st[8][:, 0:8])):
                        I(sds, sy.dma_start(
                            out=ap["db_st"][:, ii * 8:(ii + 1) * 8],
                            in_=bufv[:, 0:8]), 16)

                while pending_v:
                    pending_v.pop(0)()
                if bc + 1 < NBCHUNK:
                    emit_xt_load(bc + 1)
                W(ve, vs)
                for t in range(NTB):
                    ve.scalar_tensor_tensor(junk2, yb3[:, t, :], 1.0,
                                            yb3[:, t, :], ALU.mult, ALU.mult,
                                            accum_out=st[1][:, t:t + 1])
                    ins = ve.tensor_reduce(st[0][:, t:t + 1], yb3[:, t, :],
                                           AX.X, ALU.add)
                I(vs, ins)
                W(sc, vs)
                lastin = None
                for t in range(NTB):
                    sc.mul(st[2][:, t:t + 1], st[0][:, t:t + 1], 1.0 / DM)
                    sc.mul(st[3][:, t:t + 1], st[1][:, t:t + 1], 1.0 / DM)
                    sc.square(st[4][:, t:t + 1], st[2][:, t:t + 1])
                    sc.activation(st[4][:, t:t + 1], st[4][:, t:t + 1],
                                  AF.Identity, scale=-1.0, bias=ceps5)
                    lastin = sc.activation(st[5][:, t:t + 1],
                                           st[3][:, t:t + 1], AF.Sqrt,
                                           bias=st[4][:, t:t + 1])
                fsn = I(ss, lastin)
                W(ve, ss, fsn)
                ins = ve.reciprocal(st[6][:, 0:8], st[5][:, 0:8])
                FV(ins)
                for t in range(NTB):
                    W(ve, sdx, xpl_n[t % 2])
                    G(ve, "outb%d" % (t % 2))
                    jk = junk2 if t % 2 == 0 else junk3
                    ins = ve.tensor_scalar(jk, yb3[:, t, :], st[2][:, t:t + 1],
                                           st[6][:, t:t + 1], ALU.subtract,
                                           ALU.mult)
                    FV(ins)
                    ins = ve.tensor_tensor(
                        outb[:, (t % 2) * DM:(t % 2) * DM + DM], jk,
                        xpl[:, (t % 2) * DM:(t % 2) * DM + DM], ALU.add)
                    I(vs, ins)
                    setg("xpl%d" % (t % 2), vs)
                    W(sy, vs)
                    if t + 2 < NTB:
                        db(sy, sdx)
                        ins = sy.dma_start(
                            out=xpl[:, ((t + 2) % 2) * DM:
                                    ((t + 2) % 2) * DM + DM],
                            in_=ap["xprime"][(bc * NTB + t + 2) * 128:
                                             (bc * NTB + t + 3) * 128, :])
                        xpl_n[(t + 2) % 2] = I(sdx, ins, 16)
                    db(sy, sds)
                    ins = sy.dma_start(
                        out=ap["out"][(bc * NTB + t) * 128:
                                      (bc * NTB + t + 1) * 128, :],
                        in_=outb[:, (t % 2) * DM:(t % 2) * DM + DM])
                    I(sds, ins, 16)
                    setg("outb%d" % (t % 2), sds)

            W(sy, sds)

        engs = {}

        @block.gpsimd
        def _(g):
            engs["gpsimd"] = g
            script("gpsimd", engs)

        @block.scalar
        def _(s):
            engs["scalar"] = s
            script("scalar", engs)

        @block.tensor
        def _(t):
            engs["tensor"] = t
            script("tensor", engs)

        @block.vector
        def _(v):
            engs["vector"] = v
            script("vector", engs)

        @block.sync
        def _(s):
            engs["sync"] = s
            script("sync", engs)

    return nc



def _prep_inputs(inp):
    f = np.float32
    asc = np.ascontiguousarray

    def T(a):
        return asc(np.asarray(a, f).T)

    We1 = np.asarray(inp["L0_ffn_We1"], f)
    We3 = np.asarray(inp["L0_ffn_We3"], f)
    We2 = np.asarray(inp["L0_ffn_We2"], f)
    we2p = np.transpose(We2, (0, 2, 1))
    we2p = asc(np.concatenate([we2p[:, 1:, :], we2p[:, 0:1, :]], axis=1))
    Ws2 = np.asarray(inp["L0_ffn_Ws2"], f)
    ws2p = Ws2.T
    ws2p = asc(np.concatenate([ws2p[1:, :], ws2p[0:1, :]], axis=0))

    btw5e = np.zeros((NEXP, 4, 5, SI), f)
    for j in range(4):
        btw5e[:, j, j, :] = we2p[:, EI, :]
        btw5e[:, j, 4, :] = np.asarray(inp["L0_ffn_be2"], f)
    btw5e = asc(btw5e)
    btw5s = np.zeros((4, 5, SI), f)
    for j in range(4):
        btw5s[j, j, :] = ws2p[SI, :]
        btw5s[j, 4, :] = np.asarray(inp["L0_ffn_bs2"], f)
    btw5s = asc(btw5s)
    bt5i = np.zeros((5, 256), np.float32)
    bt5i[4, :] = 1.0
    bt5i = bt5i.astype(ml_dtypes.bfloat16)

    def lay_be(b):
        ne, it = b.shape
        nmt = (it + 127) // 128
        bp = np.zeros((ne, nmt * 128), f)
        bp[:, :it] = b
        return asc(bp.reshape(ne, nmt, 128).transpose(2, 0, 1).reshape(
            128, ne * nmt))

    def lay_bs(b):
        it = b.shape[0]
        nmt = (it + 127) // 128
        bp = np.zeros(nmt * 128, f)
        bp[:it] = b
        return asc(bp.reshape(nmt, 128).T)

    shared = {
        "wq_t": T(inp["L0_attn_Wq"]),
        "wk_t": T(inp["L0_attn_Wk"]),
        "wv_t": T(inp["L0_attn_Wv"]),
        "wp_t": T(inp["L0_attn_Wp"]),
        "bp_bc": asc(np.broadcast_to(np.asarray(inp["L0_attn_bp"], f),
                                     (128, DM))),
        "gw_t": T(inp["L0_ffn_gate_w"]),
        "we1_t": asc(np.transpose(We1, (0, 2, 1))),
        "we3_t": asc(np.transpose(We3, (0, 2, 1))),
        "we2p_t": we2p,
        "be1_l": lay_be(np.asarray(inp["L0_ffn_be1"], f)),
        "be3_l": lay_be(np.asarray(inp["L0_ffn_be3"], f)),
        "btw5e": btw5e,
        "ws1_t": T(inp["L0_ffn_Ws1"]),
        "ws3_t": T(inp["L0_ffn_Ws3"]),
        "ws2p_t": ws2p,
        "bs1_l": lay_bs(np.asarray(inp["L0_ffn_bs1"], f)),
        "bs3_l": lay_bs(np.asarray(inp["L0_ffn_bs3"], f)),
        "btw5s": btw5s,
        "mask_ut": asc(np.triu(np.ones((128, 128), f))),
        "ident_bf": np.eye(128, dtype=ml_dtypes.bfloat16),
        "bt5_init": bt5i,
        "onescol": asc(np.stack([np.ones(128), np.concatenate(
            [np.ones(127), [0.0]])], axis=1).astype(ml_dtypes.bfloat16)),
    }
    x = np.asarray(inp["L0_attn_x"], f)
    in_maps = []
    for c in range(N_CORES):
        m = dict(shared)
        m["x"] = asc(x[2 * c:2 * c + 2].reshape(NTOK, DM))
        in_maps.append(m)
    return in_maps


def kernel(**inputs):
    in_maps = _prep_inputs(inputs)
    nc = build_program(phase_b=PHASE_B)
    res = run_bass_kernel_spmd(nc, in_maps, core_ids=list(range(N_CORES)))
    out = np.concatenate(
        [r["out"].reshape(2, 2048, DM) for r in res.results], axis=0)
    return out
